# version 01 (final)
# baseline (speedup 1.0000x reference)
import contextlib
import numpy as np
import ml_dtypes
import concourse.bass as bass
import concourse.mybir as mybir
from concourse.bass_utils import run_bass_kernel_spmd

F32 = mybir.dt.float32
BF16 = mybir.dt.bfloat16
I32 = mybir.dt.int32
AF = mybir.ActivationFunctionType
ALU = mybir.AluOpType
AX = mybir.AxisListType

NCORES = 8
B, S, D = 2, 8192, 1024
SEG = 2048
T = SEG
KCH = D // 128
NT = T // 512
BFD = np.dtype(ml_dtypes.bfloat16)

ENGS = ("sync", "gpsimd", "tensor", "vector", "scalar")


def core_bs(c):
    return c % 2, c // 2


def pack_pf(v, width=128):
    v = np.asarray(v, np.float32)
    return np.ascontiguousarray(v.reshape(-1, width).T)


class LB:

    def __init__(self, name):
        self.nc = bass.Bass("TRN2", target_bir_lowering=False, debug=False)
        self.name = name
        self.ops = {e: [] for e in ENGS}
        self.cnt = {}
        self.bank_free = {}
        self.wbuf_free = {}
        self._es = contextlib.ExitStack()
        self._finalized = False

    def din(self, name, shape, dtype=F32):
        return self.nc.dram_tensor(name, list(shape), dtype, kind="ExternalInput").ap()

    def dout(self, name, shape, dtype=F32):
        return self.nc.dram_tensor(name, list(shape), dtype, kind="ExternalOutput").ap()

    def dtmp(self, name, shape, dtype=F32):
        return self.nc.dram_tensor(name, list(shape), dtype).ap()

    def sb(self, name, shape, dtype=F32):
        return self._es.enter_context(self.nc.sbuf_tensor(name, list(shape), dtype))

    def psb(self, name, shape=(128, 512), dtype=F32):
        return self._es.enter_context(self.nc.psum_tensor(name, list(shape), dtype))

    def bump(self, sem, by):
        self.cnt[sem] = self.cnt.get(sem, 0) + by
        return self.cnt[sem]

    def at(self, sem):
        return self.cnt.get(sem, 0)

    def raw(self, eng, call, wait=None, sem=None, inc=None):
        tgt = self.bump(sem, inc) if sem else None
        def f(E, SEMS, call=call, wait=wait, sem=sem, inc=inc):
            for (s, v) in (wait or []):
                if v > 0:
                    E.wait_ge(SEMS[s], v)
            ins = call(E)
            if sem:
                ins.then_inc(SEMS[sem], inc)
        for (s, _v) in (wait or []):
            self.cnt.setdefault(s, 0)
        self.ops[eng].append(f)
        return tgt

    def join(self, sems=("ve_c", "ac_c", "mm_c"),
             engs=("tensor", "vector", "scalar")):
        targets = [(x, self.at(x)) for x in sems]
        for e in engs:
            self.raw(e, lambda E: E.nop(), wait=targets)

    def dma(self, eng, out, in_, sem=None, wait=None, **kw):
        return self.raw(eng,
                        (lambda E, out=out, in_=in_, kw=kw:
                         E.dma_start(out=out, in_=in_, **kw)),
                        wait=wait, sem=sem, inc=16 if sem else None)

    def finalize(self):
        assert not self._finalized
        self._finalized = True
        sems = {}
        for s in self.cnt:
            sems[s] = self._es.enter_context(self.nc.semaphore(s))
        block = self._es.enter_context(self.nc.Block())
        for eng in ENGS:
            stream = self.ops[eng]
            if not stream:
                continue
            section = getattr(block, eng)

            def body(E, stream=stream, sems=sems):
                for fn in stream:
                    fn(E, sems)
            section(body)
        self._es.close()
        return self.nc



def emit_load_fm(lb, x_sb, x_dram, ncols, sem="ld", width=128, eng="sync"):
    R = x_dram.shape[0] // width
    for k in range(R):
        lb.dma(eng, x_sb[:, k * ncols:(k + 1) * ncols],
               x_dram[k * width:(k + 1) * width, :], sem=sem)
    return lb.at(sem)


def emit_ln3(lb, tag, x_sb, xn_sb, scr_sb, stats_sb, ones_sb, bco_sb, pbanks,
             wait_x=(), Tn=T, eps_ap=None, x_col0=0, xn_col0=0, stride=None,
             xn_stride=None, gbias=None, stats_bufs=2, x_shift=None):
    if stride is None:
        stride = Tn
    if xn_stride is None:
        xn_stride = Tn
    ps_s, ps_q, ps_a, ps_b = pbanks
    sm, sv, sa = "mm_" + tag, "ve_" + tag, "ac_" + tag
    ntt = Tn // 512
    CS = KCH * 512

    def xsl(k, tt):
        c = x_col0 + k * stride + tt * 512
        if x_shift is not None:
            c += x_shift[k]
        return x_sb[:, c:c + 512]

    def xnsl(k, tt):
        c = xn_col0 + k * xn_stride + tt * 512
        return xn_sb[:, c:c + 512]

    def srow(i, tt):
        c = (tt % stats_bufs) * 5 * 512 + i * 512
        return stats_sb[0:1, c:c + 512]

    cast = scr_sb[:, 0:CS].rearrange("p (k t) -> p k t", k=KCH)
    sq = scr_sb[:, CS:2 * CS].rearrange("p (k t) -> p k t", k=KCH)

    last_apply_ve = 0
    m_ab_hist = []
    for tt in range(ntt):
        war = [(sm, lb.at(sm))] if tt > 0 else []
        v_cast = None
        for k in range(KCH):
            v_cast = lb.raw("vector",
                            (lambda E, k=k, tt=tt: E.tensor_copy(cast[:, k, :], xsl(k, tt))),
                            wait=(list(wait_x) + war) if k == 0 else None,
                            sem=sv, inc=1)
        a_sq = lb.raw("scalar",
                      (lambda E: E.activation(sq[:, :, :], cast[:, :, :], AF.Square)),
                      wait=[(sv, v_cast)], sem=sa, inc=1)
        for k in range(KCH):
            lb.raw("tensor",
                   (lambda E, k=k: E.matmul(ps_s[0:1, :], ones_sb[:, 0:1],
                                            cast[:, k, :],
                                            start=(k == 0), stop=(k == KCH - 1))),
                   wait=[(sv, v_cast)] if k == 0 else None,
                   sem=(sm if k == KCH - 1 else None), inc=1 if k == KCH - 1 else None)
        m_sum = lb.at(sm)
        for k in range(KCH):
            lb.raw("tensor",
                   (lambda E, k=k: E.matmul(ps_q[0:1, :], ones_sb[:, 0:1],
                                            sq[:, k, :],
                                            start=(k == 0), stop=(k == KCH - 1))),
                   wait=[(sa, a_sq)] if k == 0 else None,
                   sem=(sm if k == KCH - 1 else None), inc=1 if k == KCH - 1 else None)
        m_sq = lb.at(sm)

        v_mu = lb.raw("vector", (lambda E, tt=tt: E.tensor_scalar_mul(
            srow(0, tt), ps_s[0:1, :], 1.0 / D)), wait=[(sm, m_sum)], sem=sv, inc=1)
        v_m2 = lb.raw("vector", (lambda E, tt=tt: E.tensor_scalar_mul(
            srow(1, tt), ps_q[0:1, :], 1.0 / D)), wait=[(sm, m_sq)], sem=sv, inc=1)
        lb.raw("vector", (lambda E, tt=tt: E.tensor_mul(
            srow(2, tt), srow(0, tt), srow(0, tt))), sem=sv, inc=1)
        v_var = lb.raw("vector", (lambda E, tt=tt: E.tensor_sub(
            srow(2, tt), srow(1, tt), srow(2, tt))), sem=sv, inc=1)
        war_srow = ([(sm, m_ab_hist[tt - stats_bufs])]
                    if tt >= stats_bufs else [])
        lb.raw("scalar", (lambda E, tt=tt: E.activation(
            srow(3, tt), srow(2, tt), AF.Ln, bias=eps_ap)),
            wait=[(sv, v_var)] + war_srow, sem=sa, inc=1)
        a_a = lb.raw("scalar", (lambda E, tt=tt: E.activation(
            srow(3, tt), srow(3, tt), AF.Exp, scale=-0.5)), sem=sa, inc=1)
        v_b = lb.raw("vector", (lambda E, tt=tt: E.scalar_tensor_tensor(
            srow(4, tt), srow(0, tt), -1.0, srow(3, tt), ALU.mult, ALU.mult)),
            wait=[(sa, a_a)], sem=sv, inc=1)

        war_ab = [(sv, last_apply_ve)] if last_apply_ve else []
        lb.raw("tensor", (lambda E, tt=tt: E.matmul(
            ps_a[:, :], bco_sb[0:1, :], srow(3, tt), start=True, stop=True)),
            wait=[(sa, a_a)] + war_ab, sem=sm, inc=1)
        lb.raw("tensor", (lambda E, tt=tt: E.matmul(
            ps_b[:, :], bco_sb[0:1, :], srow(4, tt), start=True, stop=True)),
            wait=[(sv, v_b)], sem=sm, inc=1)
        m_ab = lb.at(sm)
        m_ab_hist.append(m_ab)

        for k in range(KCH):
            lb.raw("vector",
                   (lambda E, k=k, tt=tt: E.tensor_tensor(
                       xnsl(k, tt), xsl(k, tt), ps_a[:, :], ALU.mult)),
                   wait=[(sm, m_ab)] if k == 0 else None, sem=sv, inc=1)
            last_apply_ve = lb.raw("vector",
                                   (lambda E, k=k, tt=tt: E.tensor_tensor(
                                       xnsl(k, tt), xnsl(k, tt), ps_b[:, :], ALU.add)),
                                   sem=sv, inc=1)
        if gbias is not None:
            g_sb, b_sb = gbias
            for k in range(KCH):
                last_apply_ve = lb.raw(
                    "vector",
                    (lambda E, k=k, tt=tt: E.tensor_scalar(
                        xnsl(k, tt), xnsl(k, tt), g_sb[:, k:k + 1], b_sb[:, k:k + 1],
                        ALU.mult, ALU.add)),
                    sem=sv, inc=1)
    return lb.at(sv), lb.at(sa)


def emit_proj(lb, tag, w_dram, wp_sb, rhs_sb, KIN, MOUT, Tn, evac, pb,
              rhs_col0=0, rhs_stride=None, m_panel=8, wait_rhs=(), w_m0=0,
              bank_free=None, rhs_bf=None, wait_w=(), rhs_shift=None):
    if rhs_stride is None:
        rhs_stride = Tn
    sm, sw = "mm_" + tag, "wt_" + tag
    KC = KIN // 128
    MC = MOUT // 128
    ntt = Tn // 512
    ngrp = max(1, len(pb) // ntt)
    npanel = (MC + m_panel - 1) // m_panel
    pwidth = m_panel * 128
    bank_free = bank_free if bank_free is not None else lb.bank_free
    for p in range(npanel):
        mp = min(m_panel, MC - p * m_panel)
        buf = p % 2
        pcol = buf * KC * pwidth
        fr = lb.wbuf_free.pop(buf, None)
        war_w = [fr] if fr is not None else []
        if p == 0:
            war_w.extend(wait_w)
        else:
            war_w.append((sw, lb.at(sw)))
        for k in range(KC):
            lb.dma("gpsimd",
                   wp_sb[:, pcol + k * pwidth: pcol + k * pwidth + mp * 128],
                   w_dram[k * 128:(k + 1) * 128,
                          w_m0 + p * m_panel * 128: w_m0 + p * m_panel * 128 + mp * 128],
                   sem=sw, wait=war_w if k == 0 else None)
        wt_done = lb.at(sw)
        for mi in range(mp):
            m = p * m_panel + mi
            grp = m % ngrp
            for k in range(KC):
                for tt in range(ntt):
                    bank = pb[grp * ntt + tt]
                    wait = []
                    if k == 0:
                        if mi == 0 and tt == 0:
                            wait.append((sw, wt_done))
                            if p == 0:
                                wait.extend(wait_rhs)
                        fr = bank_free.pop(pb[grp * ntt + tt].name, None)
                        if fr is not None:
                            wait.append(fr)
                    last = (k == KC - 1)
                    sh = rhs_shift[k] if rhs_shift is not None else 0
                    lb.raw("tensor",
                           (lambda E, k=k, tt=tt, mi=mi, pcol=pcol, bank=bank,
                            sh=sh:
                            E.matmul(bank[:, :],
                                     wp_sb[:, pcol + k * pwidth + mi * 128:
                                           pcol + k * pwidth + mi * 128 + 128],
                                     rhs_sb[:, rhs_col0 + k * rhs_stride + tt * 512 + sh:
                                            rhs_col0 + k * rhs_stride + tt * 512 + 512 + sh],
                                     start=(k == 0), stop=last)),
                           wait=wait or None,
                           sem=sm if last else None, inc=1 if last else None)
            mmval = lb.at(sm)
            for tt in range(ntt):
                bank_free[pb[grp * ntt + tt].name] = evac(m, tt, pb[grp * ntt + tt], mmval)
        lb.wbuf_free[buf] = (sm, lb.at(sm))
    return lb.at(sm)



def build_tffn():
    lb = LB("tffn")
    x_in = lb.din("x_in", [D, T])
    w12T = lb.din("w12T", [D, 2 * 4096])
    h1T = lb.din("h1T", [4096, 4096])
    h2T = lb.din("h2T", [4096, 4096])
    w3T = lb.din("w3T", [4096, D])
    s12 = lb.din("s12", [128, 64])
    sh1 = lb.din("sh1", [128, 32])
    sh2 = lb.din("sh2", [128, 32])
    s3 = lb.din("s3", [128, 8])
    y_out = lb.dout("y_out", [D, T])

    ha_d = lb.dtmp("ha_d", [32, 128, T], BF16)
    hb_d = lb.dtmp("hb_d", [32, 128, T], BF16)
    hc_d = lb.dtmp("hc_d", [32, 128, T], BF16)

    A0 = lb.sb("A0", [128, 16384], F32)
    A0b = A0[:, :].bitcast(BF16)
    A1 = lb.sb("A1", [128, 16384], BF16)
    A1f = A1[:, :].bitcast(F32)
    AW = lb.sb("AW", [128, 16384], BF16)
    SCR = lb.sb("SCR", [128, 8192], BF16)
    STATS = lb.sb("STATS", [1, 2 * 5 * 512], F32)
    ONES = lb.sb("ONES", [128, 1], BF16)
    BCO = lb.sb("BCO", [1, 128], F32)
    EPS = lb.sb("EPS", [1, 1], F32)
    T1 = lb.sb("T1", [128, 2048], BF16)
    T3 = lb.sb("T3", [128, 2048], BF16)
    SC = lb.sb("SC", [128, 64 + 32 + 32 + 8], F32)
    pb = [lb.psb(f"pb{i}") for i in range(8)]

    lb.raw("vector", lambda E: E.memset(ONES[:, :], 1.0), sem="ve_i", inc=1)
    lb.raw("vector", lambda E: E.memset(BCO[:, :], 1.0), sem="ve_i", inc=1)
    lb.raw("vector", lambda E: E.memset(EPS[:, :], 1e-5), sem="ve_i", inc=1)
    v_init = lb.at("ve_i")
    lb.dma("sync", SC[:, 0:64], s12[:, :], sem="ld")
    lb.dma("sync", SC[:, 64:96], sh1[:, :], sem="ld")
    lb.dma("sync", SC[:, 96:128], sh2[:, :], sem="ld")
    lb.dma("sync", SC[:, 128:136], s3[:, :], sem="ld")
    ld_sc = lb.at("ld")
    ld_x = emit_load_fm(lb, A0, x_in, T, sem="ld")

    ve_ln, ac_ln = emit_ln3(lb, "tffn", A0, A1, SCR, STATS, ONES, BCO,
                            pb[0:4], wait_x=[("ld", ld_x), ("ve_i", v_init)],
                            eps_ap=EPS[0:1, 0:1])

    def evac12(m, tt, bank, mmval):
        j = m // 2
        if m % 2 == 0:
            w = [("mm_t2", mmval), ("ve_t2", lb.at("ve_t2"))]
            if tt == 0 and m == 0:
                w.append(("ld", ld_sc))
            a = lb.raw("scalar",
                       (lambda E, tt=tt, bank=bank, m=m: E.activation(
                           T1[:, tt * 512:(tt + 1) * 512], bank[:, :], AF.Silu,
                           scale=SC[:, m:m + 1])),
                       wait=w, sem="ac_t2", inc=1)
            return ("ac_t2", a)
        v = lb.raw("vector",
                   (lambda E, tt=tt, bank=bank, m=m: E.scalar_tensor_tensor(
                       T3[:, tt * 512:(tt + 1) * 512], bank[:, :], SC[:, m:m + 1],
                       T1[:, tt * 512:(tt + 1) * 512], ALU.mult, ALU.mult)),
                   wait=[("mm_t2", mmval), ("ac_t2", lb.at("ac_t2")),
                         ("st_ha", lb.at("st_ha"))],
                   sem="ve_t2", inc=1)
        if tt == NT - 1:
            lb.dma("sync", ha_d[j], T3[:, :], sem="st_ha",
                   wait=[("ve_t2", v)])
        return ("ve_t2", v)

    lb.cnt.setdefault("mm_t2", 0)
    lb.cnt.setdefault("st_ha", 0)
    emit_proj(lb, "t2", w12T, AW, A1, D, 8192, T, evac12, pb,
              m_panel=8, wait_rhs=[("ve_tffn", ve_ln)])
    st_ha = lb.at("st_ha")

    def mid_layer(tag, src_d, dst_d, wT, scol0, wait_src):
        st_tag = "st_" + tag
        lb.cnt.setdefault(st_tag, 0)
        for th in range(2):
            war = [("mm_" + tag, lb.at("mm_" + tag))] if th > 0 else []
            for k in range(32):
                lb.dma("gpsimd", A0b[:, k * 1024:(k + 1) * 1024],
                       src_d[k, :, th * 1024:(th + 1) * 1024],
                       sem="ld_" + tag,
                       wait=(wait_src + war) if k == 0 else None)
            ld_half = lb.at("ld_" + tag)

            def evac(m, tt, bank, mmval, th=th, scol0=scol0, tag=tag, dst_d=dst_d):
                a = lb.raw("scalar",
                           (lambda E, tt=tt, bank=bank, m=m: E.activation(
                               T3[:, tt * 512:(tt + 1) * 512], bank[:, :], AF.Gelu,
                               scale=SC[:, scol0 + m:scol0 + m + 1])),
                           wait=[("mm_" + tag, mmval), (st_tag, lb.at(st_tag)),
                                 ("ld", ld_sc)],
                           sem="ac_" + tag, inc=1)
                if tt == 1:
                    lb.dma("sync", dst_d[m, :, th * 1024:(th + 1) * 1024],
                           T3[:, 0:1024], sem=st_tag, wait=[("ac_" + tag, a)])
                return ("ac_" + tag, a)

            emit_proj(lb, tag, wT, AW, A0b, 4096, 4096, 1024, evac, pb,
                      rhs_stride=1024, m_panel=2,
                      wait_rhs=[("ld_" + tag, ld_half)])
        return lb.at(st_tag)

    st_hb = mid_layer("h1", ha_d, hb_d, h1T, 64, [("st_ha", st_ha)])
    st_hc = mid_layer("h2", hb_d, hc_d, h2T, 96, [("st_h1", st_hb)])

    for th in range(2):
        war = [("mm_w3", lb.at("mm_w3"))] if th > 0 else []
        for k in range(32):
            lb.dma("gpsimd", A0b[:, k * 1024:(k + 1) * 1024],
                   hc_d[k, :, th * 1024:(th + 1) * 1024],
                   sem="ld_w3", wait=([("st_h2", st_hc)] + war) if k == 0 else None)
        for k in range(KCH):
            lb.dma("sync", A1f[:, k * 1024:(k + 1) * 1024],
                   x_in[k * 128:(k + 1) * 128, th * 1024:(th + 1) * 1024],
                   sem="ld_w3x", wait=[("st_h1", st_hb)] if k == 0 else None)
        ld_half = lb.at("ld_w3")
        ld_xh = lb.at("ld_w3x")

        def evac3(m, tt, bank, mmval, th=th, ld_xh=ld_xh):
            v = lb.raw("vector",
                       (lambda E, tt=tt, bank=bank, m=m: E.scalar_tensor_tensor(
                           T3[:, :].bitcast(F32)[:, tt * 512:(tt + 1) * 512],
                           bank[:, :], SC[:, 128 + m:129 + m],
                           A1f[:, m * 1024 + tt * 512: m * 1024 + tt * 512 + 512],
                           ALU.mult, ALU.add)),
                       wait=[("mm_w3", mmval), ("ld_w3x", ld_xh), ("st_y", lb.at("st_y"))],
                       sem="ve_w3", inc=1)
            lb.dma("sync", y_out[m * 128:(m + 1) * 128,
                                 th * 1024 + tt * 512: th * 1024 + tt * 512 + 512],
                   T3[:, :].bitcast(F32)[:, tt * 512:(tt + 1) * 512],
                   sem="st_y", wait=[("ve_w3", v)])
            return ("ve_w3", v)

        emit_proj(lb, "w3", w3T, AW, A0b, 4096, D, 1024, evac3, pb,
                  rhs_stride=1024, m_panel=2,
                  wait_rhs=[("ld_w3", ld_half)])

    lb.raw("sync", lambda E: E.engine_nop() if False else E.nop(),
           wait=[("st_y", lb.at("st_y"))])
    return lb.finalize()


def tffn_host_inputs(inputs):
    w1T = np.ascontiguousarray(inputs["L1_ffn_w1_w"].T)
    w2T = np.ascontiguousarray(inputs["L1_ffn_w2_w"].T)
    w12 = np.empty((D, 8192), np.float32)
    for j in range(32):
        w12[:, 256 * j:256 * j + 128] = w1T[:, 128 * j:128 * (j + 1)]
        w12[:, 256 * j + 128:256 * j + 256] = w2T[:, 128 * j:128 * (j + 1)]
    s1 = pack_pf(inputs["L1_ffn_w1_s"])
    s2 = pack_pf(inputs["L1_ffn_w2_s"])
    s12 = np.empty((128, 64), np.float32)
    s12[:, 0::2] = s1
    s12[:, 1::2] = s2
    return {
        "w12T": w12,
        "h1T": np.ascontiguousarray(inputs["L1_ffn_h1_w"].T),
        "h2T": np.ascontiguousarray(inputs["L1_ffn_h2_w"].T),
        "w3T": np.ascontiguousarray(inputs["L1_ffn_w3_w"].T),
        "s12": s12,
        "sh1": pack_pf(inputs["L1_ffn_h1_s"]),
        "sh2": pack_pf(inputs["L1_ffn_h2_s"]),
        "s3": pack_pf(inputs["L1_ffn_w3_s"]),
    }


_NC_CACHE = {}


def get_nc(name, builder):
    if name not in _NC_CACHE:
        _NC_CACHE[name] = builder()
    return _NC_CACHE[name]


def run_tffn(x_slices, inputs):
    nc = get_nc("tffn", build_tffn)
    const = tffn_host_inputs(inputs)
    in_maps = [dict(const, x_in=np.ascontiguousarray(x_slices[c]))
               for c in range(NCORES)]
    res = run_bass_kernel_spmd(nc, in_maps, core_ids=list(range(NCORES)))
    return [res.results[c]["y_out"] for c in range(NCORES)]


def slice_fm(x_full):
    out = []
    for c in range(NCORES):
        b, s = core_bs(c)
        out.append(np.ascontiguousarray(x_full[b, s * SEG:(s + 1) * SEG, :].T))
    return out


def unslice_fm(slices):
    x = np.empty((B, S, D), np.float32)
    for c in range(NCORES):
        b, s = core_bs(c)
        x[b, s * SEG:(s + 1) * SEG, :] = slices[c].T
    return x


def kernel(**inputs):
    x = np.asarray(inputs["L0_attn_a_X"], np.float32)
    xs = slice_fm(x)
    xs = run_attn_a(xs, inputs)
    xs = run_ffn0_module(xs, inputs)
    xs = run_attn_b(xs, inputs)
    xs = run_ffn0_module(xs, inputs)
    xs = run_mamba(xs, inputs)
    xs = run_tffn(xs, inputs)
    xs = run_linattn(xs, inputs)
    xs = run_tffn(xs, inputs)
    return unslice_fm(xs)



DEBUG_ATTNA = False


def build_attn_a():
    lb = LB("attn_a")
    x_in = lb.din("x_in", [D, T])
    wq = lb.din("wq", [D, D])
    wkv = lb.din("wkv", [D, 2 * D])
    bq = lb.din("bq", [128, KCH])
    bkv = lb.din("bkv", [128, 2 * KCH])
    y_out = lb.dout("y_out", [D, T])
    y_dbg = lb.dout("y_dbg", [128, 6 * T], F32) if DEBUG_ATTNA else None

    A0 = lb.sb("A0", [128, KCH * T], F32)
    A1 = lb.sb("A1", [128, KCH * T], BF16)
    AW = lb.sb("AW", [128, 2 * KCH * 128], BF16)
    SCR = lb.sb("SCR", [128, 2 * KCH * 512], BF16)
    STATS = lb.sb("STATS", [1, 2 * 5 * 512], F32)
    ONES = lb.sb("ONES", [128, 1], BF16)
    ONESR = lb.sb("ONESR", [1, 128], BF16)
    BCO = lb.sb("BCO", [1, 128], F32)
    EPS = lb.sb("EPS", [1, 1], F32)
    ID64 = lb.sb("ID64", [128, 64], BF16)
    ID128 = lb.sb("ID128", [128, 128], BF16)
    QKV = lb.sb("QKV", [128, 2 * 3 * T], BF16)
    VT = lb.sb("VT", [128, 4 * 65], BF16)
    PT = lb.sb("PT", [128, 4 * 512], BF16)
    MCOL = lb.sb("MCOL", [128, 4], BF16)
    NEGM = lb.sb("NEGM", [1, 512], BF16)
    RL = lb.sb("RL", [1, 512], F32)
    OSB = lb.sb("OSB", [64, 512], F32)
    TSB = lb.sb("TSB", [128, 512], F32)
    BIA = lb.sb("BIA", [128, 3 * KCH], F32)
    pb = [lb.psb(f"pb{i}") for i in range(8)]
    TRD = pb[1][:, :].bitcast(BF16)

    lb.raw("vector", lambda E: E.memset(ONES[:, :], 1.0), sem="ve_i", inc=1)
    lb.raw("vector", lambda E: E.memset(ONESR[:, :], 1.0), sem="ve_i", inc=1)
    lb.raw("vector", lambda E: E.memset(BCO[:, :], 1.0), sem="ve_i", inc=1)
    lb.raw("vector", lambda E: E.memset(EPS[:, :], 1e-5), sem="ve_i", inc=1)
    for g in range(4):
        lb.raw("vector", (lambda E, g=g: E.memset(
            VT[:, g * 65 + 64: g * 65 + 65], 1.0)), sem="ve_i", inc=1)
    idc = lb.nc.inline_tensor(np.concatenate([np.eye(64, dtype=BFD)] * 2, 0), "id64c").ap()
    idc2 = lb.nc.inline_tensor(np.eye(128, dtype=BFD), "id128c").ap()
    lb.dma("sync", ID64[:, :], idc[:, :], sem="ld")
    lb.dma("sync", ID128[:, :], idc2[:, :], sem="ld")
    lb.dma("sync", BIA[:, 0:KCH], bq[:, :], sem="ld")
    lb.dma("sync", BIA[:, KCH:3 * KCH], bkv[:, :], sem="ld")
    ld_b = lb.at("ld")
    v_bq = lb.raw("vector", lambda E: E.tensor_scalar_mul(
        BIA[:, 0:KCH], BIA[:, 0:KCH], 0.125), wait=[("ld", ld_b)],
        sem="ve_i", inc=1)
    v_init = lb.at("ve_i")

    ld_x = emit_load_fm(lb, A0, x_in, T, sem="ld")
    ve_ln, _ = emit_ln3(lb, "aa", A0, A1, SCR, STATS, ONES, BCO, pb[0:4],
                        wait_x=[("ld", ld_x), ("ve_i", v_init)],
                        eps_ap=EPS[0:1, 0:1])

    for s0 in ("st_y", "ve_at", "ac_at", "ac_qv", "mm_at", "mm_qv"):
        lb.cnt.setdefault(s0, 0)

    for c in range(KCH):
        buf = c % 2
        qc = buf * 3 * T
        q_sl = QKV[:, qc + 0 * T: qc + 1 * T]
        k_sl = QKV[:, qc + 1 * T: qc + 2 * T]
        v_sl = QKV[:, qc + 2 * T: qc + 3 * T]
        war_buf = [("mm_at", lb.at("mm_at"))] if c >= 2 else []

        def mk_evac(dst, bcol, scale, war):
            def evac(m, tt, bank, mmval, dst=dst, bcol=bcol, scale=scale, war=war):
                a = lb.raw("vector",
                           (lambda E, tt=tt, bank=bank: E.tensor_scalar(
                               dst[:, tt * 512:(tt + 1) * 512], bank[:, :],
                               scale, BIA[:, bcol:bcol + 1], ALU.mult, ALU.add)),
                           wait=[("mm_qv", mmval), ("ve_i", v_init)] + war,
                           sem="ac_qv", inc=1)
                return ("ac_qv", a)
            return evac

        wr = [("ve_aa", ve_ln)] if c == 0 else []
        emit_proj(lb, "qv", wq, AW, A1, D, 128, T,
                  mk_evac(q_sl, c, 0.125, war_buf), pb,
                  m_panel=1, wait_rhs=wr, w_m0=c * 128)
        emit_proj(lb, "qv", wkv, AW, A1, D, 128, T,
                  mk_evac(k_sl, KCH + c, 1.0, war_buf), pb,
                  m_panel=1, w_m0=c * 128)
        emit_proj(lb, "qv", wkv, AW, A1, D, 128, T,
                  mk_evac(v_sl, 2 * KCH + c, 1.0, war_buf), pb,
                  m_panel=1, w_m0=D + c * 128)
        ac_qkv = lb.at("ac_qv")

        for hh in range(2):
            h = 2 * c + hh
            r = h % 4
            p0 = hh * 64

            def hview(base_sl, p0=p0, r=r):
                return base_sl[p0:p0 + 64, :].rearrange(
                    "p (n s) -> p s n", s=4)[:, r, :]

            qv, kv_, vv = hview(q_sl), hview(k_sl), hview(v_sl)

            for it in range(4):
                w = [("ve_at", lb.at("ve_at"))]
                if hh == 0 and it == 0:
                    w.append(("ac_qv", ac_qkv))
                mm = lb.raw("tensor",
                            (lambda E, it=it, qv=qv, kv_=kv_: E.matmul(
                                pb[0][:, :], qv[:, it * 128:(it + 1) * 128],
                                kv_[:, :], start=True, stop=True)),
                            wait=w or None, sem="mm_at", inc=1)
                v_m = lb.raw("vector",
                             (lambda E, it=it: E.reduce_max(
                                 MCOL[:, it:it + 1], pb[0][:, :],
                                 axis=AX.X, negate=True)),
                             wait=[("mm_at", mm)], sem="ve_at", inc=1)
                lb.raw("tensor",
                       (lambda E, it=it: E.matmul(
                           pb[1][0:1, it * 128:(it + 1) * 128],
                           MCOL[:, it:it + 1], ID128[:, :],
                           start=True, stop=True)),
                       wait=[("ve_at", v_m)], sem="mm_at", inc=1)
            m_rows = lb.at("mm_at")
            v_negm = lb.raw("vector",
                            (lambda E: E.tensor_copy(NEGM[0:1, :],
                                                     pb[1][0:1, 0:512])),
                            wait=[("mm_at", m_rows)], sem="ve_at", inc=1)

            for jt in range(4):
                w = [("ac_at", lb.at("ac_at"))] if jt == 0 else None
                lb.raw("tensor",
                       (lambda E, jt=jt, qv=qv, kv_=kv_: E.matmul(
                           pb[2 + jt][:, :], kv_[:, jt * 128:(jt + 1) * 128],
                           qv[:, :], start=True, stop=False)),
                       wait=w)
                mm = lb.raw("tensor",
                            (lambda E, jt=jt: E.matmul(
                                pb[2 + jt][:, :], ONESR[0:1, :], NEGM[0:1, :],
                                start=False, stop=True)),
                            wait=[("ve_at", v_negm)] if jt == 0 else None,
                            sem="mm_at", inc=1)
                lb.raw("scalar",
                       (lambda E, jt=jt: E.activation(
                           PT[:, jt * 512:(jt + 1) * 512], pb[2 + jt][:, :],
                           AF.Exp)),
                       wait=[("mm_at", mm)], sem="ac_at", inc=1)
            ac_pt = lb.at("ac_at")

            for jt in range(4):
                w = [("ve_at", lb.at("ve_at"))] if jt == 0 else None
                mm = lb.raw("tensor",
                            (lambda E, jt=jt, vv=vv, p0=p0: E.transpose(
                                TRD[:, jt * 64:(jt + 1) * 64],
                                vv[:, jt * 128:(jt + 1) * 128],
                                ID64[p0:p0 + 64, :])),
                            wait=w, sem="mm_at", inc=1)
                lb.raw("vector",
                       (lambda E, jt=jt: E.tensor_copy(
                           VT[:, jt * 65: jt * 65 + 64],
                           TRD[:, jt * 64:(jt + 1) * 64])),
                       wait=[("mm_at", mm)], sem="ve_at", inc=1)
            v_vt = lb.at("ve_at")

            for jt in range(4):
                w = [("ve_at", v_vt), ("ac_at", ac_pt)] if jt == 0 else None
                lb.raw("tensor",
                       (lambda E, jt=jt: E.matmul(
                           pb[6][0:65, :], VT[:, jt * 65: jt * 65 + 65],
                           PT[:, jt * 512:(jt + 1) * 512],
                           start=(jt == 0), stop=(jt == 3))),
                       wait=w,
                       sem="mm_at" if jt == 3 else None,
                       inc=1 if jt == 3 else None)
            m_out = lb.at("mm_at")
            v_rl = lb.raw("vector",
                          (lambda E: E.reciprocal(RL[0:1, :], pb[6][64:65, :])),
                          wait=[("mm_at", m_out)], sem="ve_at", inc=1)
            a_o = lb.raw("scalar",
                         (lambda E: E.activation(OSB[:, :], pb[6][0:64, :],
                                                 AF.Copy)),
                         wait=[("mm_at", m_out)], sem="ac_at", inc=1)
            mm_b = lb.raw("tensor",
                          (lambda E: E.matmul(pb[7][0:64, :], BCO[0:1, 0:64],
                                              RL[0:1, :], start=True, stop=True)),
                          wait=[("ve_at", v_rl), ("ac_at", a_o)],
                          sem="mm_at", inc=1)
            v_t = lb.raw("vector",
                         (lambda E, p0=p0: E.tensor_tensor(
                             TSB[p0:p0 + 64, :], OSB[:, :],
                             pb[7][0:64, :], ALU.mult)),
                         wait=[("mm_at", mm_b)], sem="ve_at", inc=1)
            xv = A0[:, c * T:(c + 1) * T][p0:p0 + 64, :].rearrange(
                "p (n s) -> p s n", s=4)[:, r, :]
            lb.raw("vector",
                   (lambda E, xv=xv, p0=p0: E.tensor_tensor(
                       xv, xv, TSB[p0:p0 + 64, :], ALU.add)),
                   wait=[("ve_at", v_t), ("ld", ld_x)], sem="ve_at", inc=1)
        fin = ("ve_at", lb.at("ve_at"))
        for i in range(8):
            lb.bank_free[pb[i].name] = fin

    ve_fin = lb.at("ve_at")
    if DEBUG_ATTNA:
        lb.dma("gpsimd", y_dbg[:, :], QKV[:, :], sem="st_y",
               wait=[("ve_at", ve_fin), ("mm_at", lb.at("mm_at"))])
    for k in range(KCH):
        lb.dma("sync", y_out[k * 128:(k + 1) * 128, :], A0[:, k * T:(k + 1) * T],
               sem="st_y", wait=[("ve_at", ve_fin)] if k == 0 else None)
    lb.raw("sync", lambda E: E.nop(), wait=[("st_y", lb.at("st_y"))])
    return lb.finalize()


def attn_a_host_inputs(inputs):
    return {
        "wq": np.ascontiguousarray(inputs["L0_attn_a_Wq"]),
        "wkv": np.ascontiguousarray(inputs["L0_attn_a_Wkv"]),
        "bq": pack_pf(inputs["L0_attn_a_bq"]),
        "bkv": pack_pf(inputs["L0_attn_a_bkv"]),
    }


def run_attn_a(x_slices, inputs):
    nc = get_nc("attn_a", build_attn_a)
    const = attn_a_host_inputs(inputs)
    in_maps = [dict(const, x_in=np.ascontiguousarray(x_slices[c]))
               for c in range(NCORES)]
    res = run_bass_kernel_spmd(nc, in_maps, core_ids=list(range(NCORES)))
    return [res.results[c]["y_out"] for c in range(NCORES)]



def build_attn_b():
    lb = LB("attn_b")
    x_in = lb.din("x_in", [D, T])
    x_pr = lb.din("x_pr", [D, T])
    wiT = lb.din("wiT", [D, 3 * D])
    woT = lb.din("woT", [D, D])
    ibias = lb.din("ibias", [128, 24])
    gam = lb.din("gam", [128, 24])
    lal = lb.din("lal", [128, KCH])
    lga = lb.din("lga", [128, KCH])
    obias = lb.din("obias", [128, KCH])
    beta = lb.din("beta", [1])
    y_out = lb.dout("y_out", [D, T])

    A0 = lb.sb("A0", [128, KCH * T], F32)
    A0b = A0[:, :].bitcast(BF16)
    XNP = lb.sb("XNP", [128, KCH * T], BF16)
    XNO = lb.sb("XNO", [128, KCH * T], BF16)
    SCR = lb.sb("SCR", [128, 2 * KCH * 512], BF16)
    STATS = lb.sb("STATS", [1, 2 * 5 * 512], F32)
    ONES = lb.sb("ONES", [128, 1], BF16)
    BCO = lb.sb("BCO", [1, 128], F32)
    EPS = lb.sb("EPS", [1, 1], F32)
    AW = lb.sb("AW", [128, 2 * KCH * 512], BF16)
    HSELK = lb.sb("HSELK", [128, 2], BF16)
    HSELB = lb.sb("HSELB", [2, 128], BF16)
    AL = lb.sb("AL", [2, 512], BF16)
    SC = lb.sb("SC", [128, 80], F32)
    XT = lb.sb("XT", [128, 512], F32)
    YT = lb.sb("YT", [128, 512], F32)
    BET = lb.sb("BET", [128, 1], F32)
    pb = [lb.psb(f"pb{i}") for i in range(8)]

    W2c = 0
    QK = 16384

    lb.raw("vector", lambda E: E.memset(ONES[:, :], 1.0), sem="ve_i", inc=1)
    lb.raw("vector", lambda E: E.memset(BCO[:, :], 1.0), sem="ve_i", inc=1)
    lb.raw("vector", lambda E: E.memset(EPS[:, :], 1e-5), sem="ve_i", inc=1)
    hsk = np.zeros((128, 2), dtype=BFD)
    hsk[0:64, 0] = 1
    hsk[64:128, 1] = 1
    hsb = np.zeros((2, 128), dtype=BFD)
    hsb[0, 0:64] = 1
    hsb[1, 64:128] = 1
    lb.dma("sync", HSELK[:, :], lb.nc.inline_tensor(hsk, "hselk_c").ap(), sem="ld")
    lb.dma("sync", HSELB[:, :], lb.nc.inline_tensor(hsb, "hselb_c").ap(), sem="ld")
    lb.dma("sync", SC[:, 0:24], gam[:, :], sem="ld")
    lb.dma("sync", SC[:, 24:48], ibias[:, :], sem="ld")
    lb.dma("sync", SC[:, 48:48 + KCH], lal[:, :], sem="ld")
    lb.dma("sync", SC[:, 56:56 + KCH], lga[:, :], sem="ld")
    lb.dma("sync", SC[:, 64:64 + KCH], obias[:, :], sem="ld")
    lb.dma("sync", BET[:, 0], beta[:].to_broadcast([128]), sem="ld")
    ld_c = lb.at("ld")
    lb.raw("vector", lambda E: E.tensor_mul(
        SC[:, 24:48], SC[:, 24:48], SC[:, 0:24]), wait=[("ld", ld_c)],
        sem="ve_i", inc=1)
    lb.raw("vector", lambda E: E.tensor_scalar_mul(
        SC[:, 0:KCH], SC[:, 0:KCH], 0.125), sem="ve_i", inc=1)
    lb.raw("vector", lambda E: E.tensor_scalar_mul(
        SC[:, 24:24 + KCH], SC[:, 24:24 + KCH], 0.125), sem="ve_i", inc=1)
    lb.raw("vector", lambda E: E.tensor_mul(
        SC[:, 72:72 + KCH], SC[:, 64:64 + KCH], SC[:, 56:56 + KCH]),
        sem="ve_i", inc=1)
    v_init = lb.at("ve_i")

    ld_xp = emit_load_fm(lb, A0, x_pr, T, sem="ld")
    ve_lnp, _ = emit_ln3(lb, "ab", A0, XNP, SCR, STATS, ONES, BCO, pb[0:4],
                         wait_x=[("ld", ld_xp), ("ve_i", v_init)],
                         eps_ap=EPS[0:1, 0:1])
    for k in range(KCH):
        lb.dma("sync", A0[:, k * T:(k + 1) * T], x_in[k * 128:(k + 1) * 128, :],
               sem="ld", wait=[("ve_ab", ve_lnp)] if k == 0 else None)
    ld_xo = lb.at("ld")
    ve_lno, _ = emit_ln3(lb, "ab2", A0, XNO, SCR, STATS, ONES, BCO, pb[0:4],
                         wait_x=[("ld", ld_xo)], eps_ap=EPS[0:1, 0:1])

    for s0 in ("ve_c", "ac_c", "mm_c", "mm_qv", "st_y", "ld_x2"):
        lb.cnt.setdefault(s0, 0)

    def evac_gb(dst, gcol, bcol, war=()):
        def evac(m, tt, bank, mmval, dst=dst, gcol=gcol, bcol=bcol, war=war):
            a = lb.raw("vector",
                       (lambda E, tt=tt, bank=bank: E.tensor_scalar(
                           dst[:, tt * 512:(tt + 1) * 512], bank[:, :],
                           SC[:, gcol:gcol + 1], SC[:, bcol:bcol + 1],
                           ALU.mult, ALU.add)),
                       wait=[("mm_qv", mmval), ("ve_i", v_init)] + list(war),
                       sem="ve_c", inc=1)
            return ("ve_c", a)
        return evac

    for c in range(KCH):
        bq_ = A0b[:, QK + 0 * T: QK + 1 * T]
        bko = A0b[:, QK + 1 * T: QK + 2 * T]
        bvo = A0b[:, QK + 2 * T: QK + 3 * T]
        bkp = A0b[:, QK + 3 * T: QK + 4 * T]
        bvp = A0b[:, QK + 4 * T: QK + 5 * T]
        war = ([("ve_c", lb.at("ve_c")), ("mm_c", lb.at("mm_c"))] if c > 0
               else [("ve_ab2", ve_lno)])
        emit_proj(lb, "qv", wiT, AW, XNO, D, 128, T,
                  evac_gb(bq_, c, 24 + c, war), pb, m_panel=1,
                  wait_rhs=[("ve_ab2", ve_lno)] if c == 0 else (),
                  w_m0=c * 128)
        emit_proj(lb, "qv", wiT, AW, XNO, D, 128, T,
                  evac_gb(bko, 8 + c, 32 + c, war), pb, m_panel=1,
                  w_m0=D + c * 128)
        emit_proj(lb, "qv", wiT, AW, XNO, D, 128, T,
                  evac_gb(bvo, 16 + c, 40 + c, war), pb, m_panel=1,
                  w_m0=2 * D + c * 128)
        emit_proj(lb, "qv", wiT, AW, XNP, D, 128, T,
                  evac_gb(bkp, 8 + c, 32 + c, war), pb, m_panel=1,
                  wait_rhs=[("ve_ab", ve_lnp)] if c == 0 else (),
                  w_m0=D + c * 128)
        emit_proj(lb, "qv", wiT, AW, XNP, D, 128, T,
                  evac_gb(bvp, 16 + c, 40 + c, war), pb, m_panel=1,
                  w_m0=2 * D + c * 128)
        ve_qkv = lb.at("ve_c")

        v_dk = lb.raw("vector", (lambda E, a=bkp, b_=bko: E.tensor_sub(a, a, b_)),
                      wait=[("ve_c", ve_qkv)], sem="ve_c", inc=1)
        v_dv = lb.raw("vector", (lambda E, a=bvp, b_=bvo: E.tensor_sub(a, a, b_)),
                      sem="ve_c", inc=1)
        v_pr = lb.raw("vector", (lambda E, a=bko, q_=bq_, dk=bkp: E.tensor_mul(a, q_, dk)),
                      sem="ve_c", inc=1)
        for tt in range(NT):
            bd = pb[4 + (tt % 2)]
            bb = pb[6 + (tt % 2)]
            frd = lb.bank_free.pop(pb[4 + (tt % 2)].name, None)
            frb = lb.bank_free.pop(pb[6 + (tt % 2)].name, None)
            mm_d = lb.raw("tensor",
                          (lambda E, tt=tt, bd=bd, bko=bko: E.matmul(
                              bd[0:2, :], HSELK[:, :],
                              bko[:, tt * 512:(tt + 1) * 512],
                              start=True, stop=True)),
                          wait=[("ve_c", v_pr)] + ([frd] if frd else []),
                          sem="mm_c", inc=1)
            a_s = lb.raw("scalar",
                         (lambda E, bd=bd: E.activation(AL[:, :], bd[0:2, :],
                                                        AF.Sigmoid)),
                         wait=[("mm_c", mm_d), ("ac_c", lb.at("ac_c"))],
                         sem="ac_c", inc=1)
            lb.bank_free[pb[4 + (tt % 2)].name] = ("ac_c", a_s)
            mm_b = lb.raw("tensor",
                          (lambda E, bb=bb: E.matmul(
                              bb[:, :], HSELB[:, :], AL[:, :],
                              start=True, stop=True)),
                          wait=[("ac_c", a_s)] + ([frb] if frb else []),
                          sem="mm_c", inc=1)
            v_t = lb.raw("vector",
                         (lambda E, tt=tt, bb=bb, bko=bko, bvp=bvp:
                          E.tensor_tensor(bko[:, tt * 512:(tt + 1) * 512],
                                          bvp[:, tt * 512:(tt + 1) * 512],
                                          bb[:, :], ALU.mult)),
                         wait=[("mm_c", mm_b)], sem="ve_c", inc=1)
            lb.bank_free[pb[6 + (tt % 2)].name] = ("ve_c", v_t)
            v_u = lb.raw("vector",
                         (lambda E, tt=tt, bko=bko, bvo=bvo: E.scalar_tensor_tensor(
                             bko[:, tt * 512:(tt + 1) * 512],
                             bko[:, tt * 512:(tt + 1) * 512], BET[:, 0:1],
                             bvo[:, tt * 512:(tt + 1) * 512], ALU.mult, ALU.add)),
                         sem="ve_c", inc=1)
            lb.raw("vector",
                   (lambda E, tt=tt, c=c, bko=bko: E.tensor_scalar_mul(
                       A0b[:, W2c + c * T + tt * 512: W2c + c * T + tt * 512 + 512],
                       bko[:, tt * 512:(tt + 1) * 512], SC[:, 48 + c:49 + c])),
                   sem="ve_c", inc=1)
    ve_w2 = lb.at("ve_c")

    def evac_out(m, tt, bank, mmval):
        ld = lb.dma("sync", XT[:, :],
                    x_in[m * 128:(m + 1) * 128, tt * 512:(tt + 1) * 512],
                    sem="ld_x2", wait=[("ve_c", lb.at("ve_c"))])
        v1 = lb.raw("vector",
                    (lambda E, m=m, bank=bank: E.tensor_scalar(
                        YT[:, :], bank[:, :], SC[:, 56 + m:57 + m],
                        SC[:, 72 + m:73 + m], ALU.mult, ALU.add)),
                    wait=[("mm_o", mmval), ("st_y", lb.at("st_y"))],
                    sem="ve_c", inc=1)
        v2 = lb.raw("vector",
                    (lambda E: E.tensor_add(YT[:, :], YT[:, :], XT[:, :])),
                    wait=[("ld_x2", ld)], sem="ve_c", inc=1)
        lb.dma("sync", y_out[m * 128:(m + 1) * 128, tt * 512:(tt + 1) * 512],
               YT[:, :], sem="st_y", wait=[("ve_c", v2)])
        return ("ve_c", v2)

    emit_proj(lb, "o", woT, AW, A0b, D, D, T, evac_out, pb,
              rhs_col0=W2c, rhs_stride=T, m_panel=4,
              wait_rhs=[("ve_c", ve_w2)])
    lb.raw("sync", lambda E: E.nop(), wait=[("st_y", lb.at("st_y"))])
    return lb.finalize()


def attn_b_host_inputs(inputs):
    return {
        "wiT": np.ascontiguousarray(inputs["L0_attn_b_in_proj_weight"].T),
        "woT": np.ascontiguousarray(inputs["L0_attn_b_out_proj_weight"].T),
        "ibias": pack_pf(inputs["L0_attn_b_in_proj_bias"]),
        "gam": pack_pf(inputs["L0_attn_b_gamma"].reshape(-1)),
        "lal": pack_pf(inputs["L0_attn_b_linear_alpha"].reshape(-1)),
        "lga": pack_pf(inputs["L0_attn_b_linear_gamma"].reshape(-1)),
        "obias": pack_pf(inputs["L0_attn_b_out_proj_bias"]),
    }


def run_attn_b(x_slices, inputs):
    nc = get_nc("attn_b", build_attn_b)
    const = attn_b_host_inputs(inputs)
    in_maps = []
    for c in range(NCORES):
        b, s = core_bs(c)
        partner = 2 * s + (1 - b)
        in_maps.append(dict(const,
                            x_in=np.ascontiguousarray(x_slices[c]),
                            x_pr=np.ascontiguousarray(x_slices[partner]),
                            beta=np.array([float(b)], np.float32)))
    res = run_bass_kernel_spmd(nc, in_maps, core_ids=list(range(NCORES)))
    return [res.results[c]["y_out"] for c in range(NCORES)]



def build_linattn():
    lb = LB("linattn")
    x_in = lb.din("x_in", [D, T])
    x_all = lb.din("x_all", [D, 4 * T])
    wqT = lb.din("wqT", [D, D])
    wkT = lb.din("wkT", [D, D])
    wvT = lb.din("wvT", [D, D])
    woT = lb.din("woT", [D, D])
    r2 = lb.din("r2", [128, 64])
    bq8 = lb.din("bq8", [128, KCH])
    bk8 = lb.din("bk8", [128, KCH])
    bv = lb.din("bv", [128, KCH])
    bo = lb.din("bo", [128, KCH])
    y_out = lb.dout("y_out", [D, T])

    A0 = lb.sb("A0", [128, KCH * T], F32)
    A0b = A0[:, :].bitcast(BF16)
    XN = lb.sb("XN", [128, KCH * T], BF16)
    SCR = lb.sb("SCR", [128, 2 * KCH * 512], BF16)
    STATS = lb.sb("STATS", [1, 2 * 5 * 512], F32)
    ONES = lb.sb("ONES", [128, 1], BF16)
    ONESR = lb.sb("ONESR", [1, 128], BF16)
    ZROW = lb.sb("ZROW", [1, 512], BF16)
    BCO = lb.sb("BCO", [1, 128], F32)
    EPS = lb.sb("EPS", [1, 1], F32)
    AW = lb.sb("AW", [128, 2 * KCH * 512], BF16)
    R2S = lb.sb("R2S", [128, 64], BF16)
    HSELK = lb.sb("HSELK", [128, 2], BF16)
    HSELB2 = lb.sb("HSELB2", [2, 128], BF16)
    ID128 = lb.sb("ID128", [128, 128], BF16)
    IDD = lb.sb("IDD", [64, 128], BF16)
    KS = lb.sb("KS", [128, T], BF16)
    VS = lb.sb("VS", [128, T], BF16)
    KSQ = lb.sb("KSQ", [128, T], BF16)
    KP = lb.sb("KP", [128, T], BF16)
    KT = lb.sb("KT", [128, 128], BF16)
    VT = lb.sb("VT", [128, 128], BF16)
    SQH = lb.sb("SQH", [1, 512], F32)
    SQH1 = lb.sb("SQH1", [1, 2 * 512], BF16)
    ZB = lb.sb("ZB", [2, 512], BF16)
    KV = lb.sb("KV", [64, 1024], F32)
    KVH = lb.sb("KVH", [64, 1024], BF16)
    KVB = lb.sb("KVB", [128, 1024], BF16)
    SC = lb.sb("SC", [128, 4 * KCH], F32)
    XT = lb.sb("XT", [128, 512], F32)
    YT = lb.sb("YT", [128, 512], F32)
    pb = [lb.psb(f"pb{i}") for i in range(8)]
    TRD = pb[5][:, :].bitcast(BF16)

    lb.raw("vector", lambda E: E.memset(ONES[:, :], 1.0), sem="ve_i", inc=1)
    lb.raw("vector", lambda E: E.memset(ONESR[:, :], 1.0), sem="ve_i", inc=1)
    lb.raw("vector", lambda E: E.memset(ZROW[:, :], 0.0), sem="ve_i", inc=1)
    lb.raw("vector", lambda E: E.memset(BCO[:, :], 1.0), sem="ve_i", inc=1)
    lb.raw("vector", lambda E: E.memset(EPS[:, :], 1e-5), sem="ve_i", inc=1)
    hsk = np.zeros((128, 2), dtype=BFD)
    hsk[0:64, 0] = 1
    hsk[64:128, 1] = 1
    lb.dma("sync", HSELK[:, :], lb.nc.inline_tensor(hsk, "la_hselk").ap(), sem="ld")
    hsb2 = np.zeros((2, 128), dtype=BFD)
    hsb2[0, 0:64] = 1
    hsb2[1, 64:128] = 1
    lb.dma("sync", HSELB2[:, :], lb.nc.inline_tensor(hsb2, "la_hselb2").ap(), sem="ld")
    lb.dma("sync", ID128[:, :], lb.nc.inline_tensor(np.eye(128, dtype=BFD), "la_id128").ap(), sem="ld")
    lb.dma("sync", IDD[:, :], lb.nc.inline_tensor(
        np.concatenate([np.eye(64, dtype=BFD)] * 2, 1), "la_idd").ap(), sem="ld")
    lb.dma("gpsimd", R2S[:, :], r2[:, :], sem="ldg")
    lb.dma("sync", SC[:, 0:KCH], bq8[:, :], sem="ld")
    lb.dma("sync", SC[:, KCH:2 * KCH], bk8[:, :], sem="ld")
    lb.dma("sync", SC[:, 2 * KCH:3 * KCH], bv[:, :], sem="ld")
    lb.dma("sync", SC[:, 3 * KCH:4 * KCH], bo[:, :], sem="ld")
    ld_c0 = lb.at("ld")
    lb.raw("vector", lambda E: E.tensor_scalar_mul(
        SC[:, 0:2 * KCH], SC[:, 0:2 * KCH], 0.125),
        wait=[("ld", ld_c0)], sem="ve_i", inc=1)
    ld_c = lb.at("ld")
    v_init = lb.at("ve_i")

    for s0 in ("ve_c", "ac_c", "mm_c", "mm_p", "st_y", "ld_x2", "ve_lq", "ve_lo"):
        lb.cnt.setdefault(s0, 0)

    for i in (6, 7):
        lb.raw("tensor",
               (lambda E, i=i: E.matmul(pb[i][0:64, :], ONESR[0:1, 0:64],
                                        ZROW[0:1, :], start=True, stop=False,
                                        skip_group_check=True)),
               wait=[("ve_i", v_init)], sem="mm_c", inc=1)

    def featuremap(SRC, SSQ, DST, base_ready):
        a_sq = lb.raw("scalar",
                      (lambda E, SRC=SRC, SSQ=SSQ: E.activation(
                          SSQ[:, :], SRC[:, :], AF.Square)),
                      wait=base_ready, sem="ac_c", inc=1)
        for tt in range(NT):
            for hh in range(2):
                p0 = hh * 64
                mm_n = lb.raw("tensor",
                              (lambda E, tt=tt, hh=hh, SSQ=SSQ: E.matmul(
                                  pb[4][0:1, :], HSELK[:, hh:hh + 1],
                                  SSQ[:, tt * 512:(tt + 1) * 512],
                                  start=True, stop=True,
                                  skip_group_check=True)),
                              wait=[("ac_c", a_sq), ("ve_c", lb.at("ve_c"))],
                              sem="mm_c", inc=1)
                lb.raw("vector",
                       (lambda E: E.tensor_scalar_mul(
                           SQH[0:1, :], pb[4][0:1, :], -0.5)),
                       wait=[("mm_c", mm_n)], sem="ve_c", inc=1)
                lb.raw("vector",
                       (lambda E: E.tensor_copy(SQH1[0:1, 0:512], SQH[0:1, :])),
                       sem="ve_c", inc=1)
                lb.raw("vector",
                       (lambda E: E.tensor_sub(SQH[0:1, :], SQH[0:1, :],
                                               SQH1[0:1, 0:512])),
                       sem="ve_c", inc=1)
                v_s3 = lb.raw("vector",
                              (lambda E: E.tensor_copy(SQH1[0:1, 512:1024],
                                                       SQH[0:1, :])),
                              sem="ve_c", inc=1)
                lb.raw("tensor",
                       (lambda E, tt=tt, p0=p0, SRC=SRC: E.matmul(
                           pb[5][0:64, :], R2S[p0:p0 + 64, :],
                           SRC[p0:p0 + 64, tt * 512:(tt + 1) * 512],
                           start=True, stop=False, skip_group_check=True)),
                       wait=[("ac_c", lb.at("ac_c")), ("ldg", 16)])
                lb.raw("tensor",
                       (lambda E: E.matmul(
                           pb[5][0:64, :], ONESR[0:1, 0:64], SQH1[0:1, 0:512],
                           start=False, stop=False, skip_group_check=True)),
                       wait=[("ve_c", v_s3)])
                mm_r = lb.raw("tensor",
                              (lambda E: E.matmul(
                                  pb[5][0:64, :], ONESR[0:1, 0:64],
                                  SQH1[0:1, 512:1024],
                                  start=False, stop=True,
                                  skip_group_check=True)),
                              sem="mm_c", inc=1)
                lb.raw("scalar",
                       (lambda E, tt=tt, p0=p0, DST=DST: E.activation(
                           DST[p0:p0 + 64, tt * 512:(tt + 1) * 512],
                           pb[5][0:64, :], AF.Exp)),
                       wait=[("mm_c", mm_r)], sem="ac_c", inc=1)
        return lb.at("ac_c")

    def evac_sb(dst, scale, bcol, war=()):
        def evac(m, tt, bank, mmval, dst=dst, scale=scale, bcol=bcol, war=war):
            a = lb.raw("vector",
                       (lambda E, tt=tt, bank=bank: E.tensor_scalar(
                           dst[:, tt * 512:(tt + 1) * 512], bank[:, :],
                           scale, SC[:, bcol:bcol + 1], ALU.mult, ALU.add)),
                       wait=[("mm_p", mmval)] + list(war), sem="ve_c", inc=1)
            return ("ve_c", a)
        return evac

    for qq in range(4):
        for k in range(KCH):
            lb.dma("sync", A0[:, k * T:(k + 1) * T],
                   x_all[k * 128:(k + 1) * 128, qq * T:(qq + 1) * T],
                   sem="ld", wait=[("ve_lq", lb.at("ve_lq"))] if k == 0 else None)
        ld_q = lb.at("ld")
        ve_ln, _ = emit_ln3(lb, "lq", A0, XN, SCR, STATS, ONES, BCO, pb[0:4],
                            wait_x=[("ld", ld_q), ("ve_i", v_init),
                                    ("mm_p", lb.at("mm_p"))],
                            eps_ap=EPS[0:1, 0:1])
        for c in range(KCH):
            war = [("ac_c", lb.at("ac_c")), ("mm_c", lb.at("mm_c"))]
            emit_proj(lb, "p", wkT, AW, XN, D, 128, T,
                      evac_sb(KS, 0.125, KCH + c, war), pb[0:4], m_panel=1,
                      wait_rhs=[("ve_lq", ve_ln), ("ld", ld_c)], w_m0=c * 128)
            emit_proj(lb, "p", wvT, AW, XN, D, 128, T,
                      evac_sb(VS, 1.0, 2 * KCH + c, war), pb[0:4], m_panel=1,
                      w_m0=c * 128)
            ve_kv = lb.at("ve_c")
            fm_done = featuremap(KS, KSQ, KP, [("ve_c", ve_kv)])
            for ti in range(T // 128):
                w0 = [("ve_c", lb.at("ve_c")), ("ac_c", fm_done)]
                m1 = lb.raw("tensor",
                            (lambda E, ti=ti: E.transpose(
                                TRD[:, 0:128], KP[:, ti * 128:(ti + 1) * 128],
                                ID128[:, :])),
                            wait=w0, sem="mm_c", inc=1)
                v1 = lb.raw("vector",
                            (lambda E: E.tensor_copy(KT[:, :], TRD[:, 0:128])),
                            wait=[("mm_c", m1)], sem="ve_c", inc=1)
                m2 = lb.raw("tensor",
                            (lambda E, ti=ti: E.transpose(
                                TRD[:, 128:256], VS[:, ti * 128:(ti + 1) * 128],
                                ID128[:, :])),
                            wait=[("ve_c", v1)], sem="mm_c", inc=1)
                v2 = lb.raw("vector",
                            (lambda E: E.tensor_copy(VT[:, :], TRD[:, 128:256])),
                            wait=[("mm_c", m2)], sem="ve_c", inc=1)
                for hh in range(2):
                    h = 2 * c + hh
                    bank = pb[6 + h // 8]
                    col = (h % 8) * 64
                    lb.raw("tensor",
                           (lambda E, hh=hh, col=col, bank=bank: E.matmul(
                               bank[0:64, col:col + 64],
                               KT[:, hh * 64:(hh + 1) * 64],
                               VT[:, hh * 64:(hh + 1) * 64],
                               start=False, stop=False,
                               skip_group_check=True)),
                           wait=[("ve_c", v2)] if hh == 0 else None,
                           sem="mm_c", inc=1)
    mm_kv = lb.at("mm_c")

    a1 = lb.raw("scalar", lambda E: E.activation(KV[:, 0:512], pb[6][0:64, :],
                                                 AF.Copy),
                wait=[("mm_c", mm_kv)], sem="ac_c", inc=1)
    a2 = lb.raw("scalar", lambda E: E.activation(KV[:, 512:1024], pb[7][0:64, :],
                                                 AF.Copy), sem="ac_c", inc=1)
    v_kvh = lb.raw("vector", lambda E: E.tensor_copy(KVH[:, :], KV[:, :]),
                   wait=[("ac_c", a2)], sem="ve_c", inc=1)
    for half in range(2):
        mmd = lb.raw("tensor",
                     (lambda E, half=half: E.matmul(
                         pb[6 + half][:, :], IDD[:, :],
                         KVH[:, half * 512:(half + 1) * 512],
                         start=True, stop=True)),
                     wait=[("ve_c", v_kvh), ("ac_c", a2)], sem="mm_c", inc=1)
        lb.raw("vector",
               (lambda E, half=half: E.tensor_copy(
                   KVB[:, half * 512:(half + 1) * 512], pb[6 + half][:, :])),
               wait=[("mm_c", mmd)], sem="ve_c", inc=1)
    ve_kvb = lb.at("ve_c")

    for k in range(KCH):
        lb.dma("sync", A0[:, k * T:(k + 1) * T], x_in[k * 128:(k + 1) * 128, :],
               sem="ld", wait=[("ve_lq", lb.at("ve_lq"))] if k == 0 else None)
    ld_o = lb.at("ld")
    ve_lno, _ = emit_ln3(lb, "lo", A0, XN, SCR, STATS, ONES, BCO, pb[0:4],
                         wait_x=[("ld", ld_o), ("mm_p", lb.at("mm_p"))],
                         eps_ap=EPS[0:1, 0:1])
    OF = A0b
    for c in range(KCH):
        war = [("ac_c", lb.at("ac_c")), ("mm_c", lb.at("mm_c"))]
        emit_proj(lb, "p", wqT, AW, XN, D, 128, T,
                  evac_sb(KS, 0.125, c, war), pb[0:4], m_panel=1,
                  wait_rhs=[("ve_lo", ve_lno)], w_m0=c * 128)
        ve_q = lb.at("ve_c")
        fm_done = featuremap(KS, KSQ, KP, [("ve_c", ve_q)])
        for tt in range(NT):
            mm_z = lb.raw("tensor",
                          (lambda E, tt=tt: E.matmul(
                              pb[4][0:2, :], HSELK[:, :],
                              KP[:, tt * 512:(tt + 1) * 512],
                              start=True, stop=True, skip_group_check=True)),
                          wait=[("ac_c", fm_done), ("ve_c", lb.at("ve_c"))],
                          sem="mm_c", inc=1)
            lb.raw("vector",
                   (lambda E: E.tensor_scalar_add(KV[0:2, 0:512],
                                                  pb[4][0:2, :], 1e-8)),
                   wait=[("mm_c", mm_z)], sem="ve_c", inc=1)
            lb.raw("vector",
                   (lambda E: E.reciprocal(KV[0:2, 0:512], KV[0:2, 0:512])),
                   sem="ve_c", inc=1)
            v_z3 = lb.raw("vector",
                          (lambda E: E.tensor_copy(ZB[:, :], KV[0:2, 0:512])),
                          sem="ve_c", inc=1)
            mm_b = lb.raw("tensor",
                          (lambda E: E.matmul(
                              pb[4][:, :], HSELB2[:, :], ZB[:, :],
                              start=True, stop=True, skip_group_check=True)),
                          wait=[("ve_c", v_z3)], sem="mm_c", inc=1)
            v_qz = lb.raw("vector",
                          (lambda E, tt=tt: E.tensor_tensor(
                              KP[:, tt * 512:(tt + 1) * 512],
                              KP[:, tt * 512:(tt + 1) * 512],
                              pb[4][:, :], ALU.mult)),
                          wait=[("mm_c", mm_b)], sem="ve_c", inc=1)
            for hh in range(2):
                h = 2 * c + hh
                p0 = hh * 64
                bank = pb[6 + hh]
                mm_o = lb.raw("tensor",
                              (lambda E, tt=tt, hh=hh, p0=p0, h=h, bank=bank:
                               E.matmul(bank[0:64, :],
                                        KVB[p0:p0 + 64, h * 64:(h + 1) * 64],
                                        KP[p0:p0 + 64, tt * 512:(tt + 1) * 512],
                                        start=True, stop=True,
                                        skip_group_check=True)),
                              wait=[("ve_c", v_qz),
                                    ("ac_c", lb.at("ac_c"))],
                              sem="mm_c", inc=1)
                lb.raw("scalar",
                       (lambda E, tt=tt, p0=p0, c=c, bank=bank: E.activation(
                           OF[p0:p0 + 64, c * T + tt * 512: c * T + tt * 512 + 512],
                           bank[0:64, :], AF.Copy)),
                       wait=[("mm_c", mm_o)], sem="ac_c", inc=1)
    ac_of = lb.at("ac_c")

    def evac_out(m, tt, bank, mmval):
        ld = lb.dma("sync", XT[:, :],
                    x_in[m * 128:(m + 1) * 128, tt * 512:(tt + 1) * 512],
                    sem="ld_x2", wait=[("ve_c", lb.at("ve_c"))])
        v1 = lb.raw("vector",
                    (lambda E, m=m, bank=bank: E.tensor_scalar(
                        YT[:, :], bank[:, :], 1.0, SC[:, 3 * KCH + m:3 * KCH + m + 1],
                        ALU.mult, ALU.add)),
                    wait=[("mm_o2", mmval), ("st_y", lb.at("st_y"))],
                    sem="ve_c", inc=1)
        v2 = lb.raw("vector",
                    (lambda E: E.tensor_add(YT[:, :], YT[:, :], XT[:, :])),
                    wait=[("ld_x2", ld)], sem="ve_c", inc=1)
        lb.dma("sync", y_out[m * 128:(m + 1) * 128, tt * 512:(tt + 1) * 512],
               YT[:, :], sem="st_y", wait=[("ve_c", v2)])
        return ("ve_c", v2)

    emit_proj(lb, "o2", woT, AW, OF, D, D, T, evac_out, pb[0:4],
              m_panel=4, wait_rhs=[("ac_c", ac_of)])
    lb.raw("sync", lambda E: E.nop(), wait=[("st_y", lb.at("st_y"))])
    return lb.finalize()


def linattn_host_inputs(inputs):
    bq8 = pack_pf(inputs["L1_attn_b_bq"])
    bk8 = pack_pf(inputs["L1_attn_b_bk"])
    return {
        "wqT": np.ascontiguousarray(inputs["L1_attn_b_Wq"].T),
        "wkT": np.ascontiguousarray(inputs["L1_attn_b_Wk"].T),
        "wvT": np.ascontiguousarray(inputs["L1_attn_b_Wv"].T),
        "woT": np.ascontiguousarray(inputs["L1_attn_b_Wo"].T),
        "r2": np.ascontiguousarray(
            np.concatenate([inputs["L1_attn_b_R"]] * 2, 0).astype(np.float32)),
        "bq8": bq8, "bk8": bk8,
        "bv": pack_pf(inputs["L1_attn_b_bv"]),
        "bo": pack_pf(inputs["L1_attn_b_bo"]),
    }


def run_linattn(x_slices, inputs):
    nc = get_nc("linattn", build_linattn)
    const = linattn_host_inputs(inputs)
    in_maps = []
    for c in range(NCORES):
        b, s = core_bs(c)
        xa = np.concatenate([x_slices[2 * ss + b] for ss in range(4)], axis=1)
        in_maps.append(dict(const,
                            x_in=np.ascontiguousarray(x_slices[c]),
                            x_all=np.ascontiguousarray(xa)))
    res = run_bass_kernel_spmd(nc, in_maps, core_ids=list(range(NCORES)))
    return [res.results[c]["y_out"] for c in range(NCORES)]



MAMBA_DBG = False
MAMBA_SYNC = 0
MAMBA_PHASE = 99
MB_ = 1024
NBLK = 8192 // MB_
OWNB = 2048 // MB_


def build_mamba():
    lb = LB("mamba")
    xs_d_in = [lb.din("xs_f", [D, 8192]), lb.din("xs_b", [D, 8192])]
    msk_in = [lb.din("mask_f", [1, 8192]), lb.din("mask_b", [1, 8192])]
    ipT = [lb.din(f"ipT{d}", [D, 2432]) for d in range(2)]
    wzT = [lb.din(f"wzT{d}", [D, 2048]) for d in range(2)]
    cw_in = [lb.din(f"cw{d}", [128, 18 * 4]) for d in range(2)]
    cb_in = [lb.din(f"cb{d}", [128, 18]) for d in range(2)]
    dtb_in = [lb.din(f"dtb{d}", [32, 1]) for d in range(2)]
    alog_in = [lb.din(f"alog{d}", [32, 1]) for d in range(2)]
    drow_in = [lb.din(f"drow{d}", [1, 32]) for d in range(2)]
    rmsw_in = [lb.din(f"rmsw{d}", [128, 16]) for d in range(2)]
    opT = [lb.din(f"opT{d}", [2048, D]) for d in range(2)]
    pwT = lb.din("pwT", [2048, D])
    pbias = lb.din("pbias", [128, KCH])
    lnw = lb.din("lnw", [128, KCH])
    lnb = lb.din("lnb", [128, KCH])
    y_out = lb.dout("y_out", [D, T])

    if MAMBA_DBG:
        yfm_d = [lb.dout(f"yfm_d{d}", [16, 128, T], BF16) for d in range(2)]
        of_d = [lb.dout(f"of_d{d}", [KCH, 128, T], BF16) for d in range(2)]
    else:
        yfm_d = [lb.dtmp(f"yfm_d{d}", [16, 128, T], BF16) for d in range(2)]
        of_d = [lb.dtmp(f"of_d{d}", [KCH, 128, T], BF16) for d in range(2)]
    if MAMBA_DBG:
        ops_d = [lb.dout(f"ops_d{d}", [16, 128, D], BF16) for d in range(2)]
    else:
        ops_d = [lb.dtmp(f"ops_d{d}", [16, 128, D], BF16) for d in range(2)]

    A0 = lb.sb("A0", [128, KCH * MB_], F32)
    A0b = A0[:, :].bitcast(BF16)
    XN = lb.sb("XN", [128, KCH * MB_], BF16)
    U = lb.sb("U", [128, 19 * (MB_ + 4)], BF16)
    SCR = lb.sb("SCR", [128, 2 * KCH * 512], BF16)
    SCRF = SCR[:, :].bitcast(F32)
    STATS = lb.sb("STATS", [1, 5 * 512], F32)
    ONES = lb.sb("ONES", [128, 1], BF16)
    ONESR = lb.sb("ONESR", [1, 128], BF16)
    ONESRF = lb.sb("ONESRF", [1, 128], F32)
    ONE128F = lb.sb("ONE128F", [128, 128], F32)
    BCO = lb.sb("BCO", [1, 128], F32)
    EPS = lb.sb("EPS", [1, 1], F32)
    ONEC = lb.sb("ONEC", [1, 1], F32)
    ONE32 = lb.sb("ONE32", [32, 1], F32)
    EPS128 = lb.sb("EPS128", [128, 1], F32)
    ID32 = lb.sb("ID32", [32, 32], F32)
    ID128 = lb.sb("ID128", [128, 128], BF16)
    AW = lb.sb("AW", [128, 12288], BF16)
    CWS = lb.sb("CWS", [128, 18 * 4], F32)
    CBS = lb.sb("CBS", [128, 18], F32)
    MSKB = lb.sb("MSKB", [1, MB_], F32)
    DTL = lb.sb("DTL", [32, MB_], F32)
    CUM = lb.sb("CUM", [32, MB_ + 1], F32)
    WL = lb.sb("WL", [32, MB_], F32)
    L1_ = lb.sb("L1_", [32, MB_], F32)
    LONE = lb.sb("LONE", [32, MB_], F32)
    ACOL = lb.sb("ACOL", [32, 1], F32)
    CARRY = lb.sb("CARRY", [32, 1], F32)
    COL1 = lb.sb("COL1", [32, 1], F32)
    ROW32 = lb.sb("ROW32", [1, 32], F32)
    EROW = lb.sb("EROW", [1, 512], F32)
    STASH = lb.sb("STASH", [128, 18 * 3], BF16)
    S_ = lb.sb("S_", [64, 2048], F32)
    SB_ = lb.sb("SB_", [64, 2048], BF16)
    XT2 = lb.sb("XT2", [128, 128], BF16)
    XW = lb.sb("XW", [128, 128], BF16)
    BT = lb.sb("BT", [128, 64], BF16)
    WT32 = lb.sb("WT32", [128, 32], F32)
    CT32 = lb.sb("CT32", [128, 32], F32)
    DTT32 = lb.sb("DTT32", [128, 32], F32)
    E1T32 = lb.sb("E1T32", [128, 32], F32)
    GSB = lb.sb("GSB", [128, 128], BF16)
    ESB = lb.sb("ESB", [128, 512], BF16)
    MSB = lb.sb("MSB", [128, 32 * 128], BF16)
    M0M = None
    DGA = None
    Y1S = lb.sb("Y1S", [128, 64], F32)
    YFMC = lb.sb("YFMC", [128, 2048], BF16)
    YFM = lb.sb("YFM", [128, 16 * 128], BF16)
    ZS = lb.sb("ZS", [128, 2048], BF16)
    YTT = SCRF[:, 2048:4096]
    M0M = SCRF[:, 0:512]
    DGA = SCRF[:, 512:1024]
    DCOL = lb.sb("DCOL", [128, 32], F32)
    MTRI = lb.sb("MTRI", [128, 128], F32)
    SQC = lb.sb("SQC", [128, 1], F32)
    VEC = lb.sb("VEC", [128, 64], F32)
    XT = lb.sb("XT", [128, 512], F32)
    YT = lb.sb("YT", [128, 512], F32)
    pb = [lb.psb(f"pb{i}") for i in range(8)]
    TRD = pb[5][:, :].bitcast(BF16)
    TRF = pb[5]

    lb.raw("vector", lambda E: E.memset(ONES[:, :], 1.0), sem="ve_i", inc=1)
    lb.raw("vector", lambda E: E.memset(ONESR[:, :], 1.0), sem="ve_i", inc=1)
    lb.raw("vector", lambda E: E.memset(ONESRF[:, :], 1.0), sem="ve_i", inc=1)
    lb.raw("vector", lambda E: E.memset(ONE128F[:, :], 1.0), sem="ve_i", inc=1)
    lb.raw("vector", lambda E: E.memset(BCO[:, :], 1.0), sem="ve_i", inc=1)
    lb.raw("vector", lambda E: E.memset(EPS[:, :], 1e-5), sem="ve_i", inc=1)
    lb.raw("vector", lambda E: E.memset(ONEC[:, :], 1.0), sem="ve_i", inc=1)
    lb.raw("vector", lambda E: E.memset(ONE32[:, :], 1.0), sem="ve_i", inc=1)
    lb.raw("vector", lambda E: E.memset(EPS128[:, :], 1e-5), sem="ve_i", inc=1)
    lb.raw("vector", lambda E: E.memset(LONE[:, :], 1.0), sem="ve_i", inc=1)
    lb.dma("sync", ID32[:, :], lb.nc.inline_tensor(
        np.eye(32, dtype=np.float32), "mm_id32").ap(), sem="ld")
    lb.dma("sync", ID128[:, :], lb.nc.inline_tensor(
        np.eye(128, dtype=BFD), "mm_id128").ap(), sem="ld")
    mtri = np.where(np.arange(128)[:, None] <= np.arange(128)[None, :],
                    0.0, -1e30).astype(np.float32)
    lb.dma("sync", MTRI[:, :], lb.nc.inline_tensor(mtri, "mm_mtri").ap(), sem="ld")
    ld_i = lb.at("ld")
    v_init = lb.at("ve_i")

    for s0 in ("ve_c", "ac_c", "mm_c", "mm_p", "mm_z", "mm_o", "st_y",
               "ld_b", "ve_lnm", "ac_lnm", "mm_lnm", "ld_x2", "ve_lnf",
               "ve_lnh", "st_f", "mm_f"):
        lb.cnt.setdefault(s0, 0)

    def transpose_to(dst_ap, src_ap, ident_ap, kdim, mdim, fp):
        tview = TRF[0:mdim, 0:kdim] if fp else TRD[0:mdim, 0:kdim]
        m = lb.raw("tensor",
                   (lambda E, tview=tview, src_ap=src_ap, ident_ap=ident_ap:
                    E.transpose(tview, src_ap, ident_ap)),
                   wait=[("ve_c", lb.at("ve_c")), ("ac_c", lb.at("ac_c"))],
                   sem="mm_c", inc=1)
        v = lb.raw("vector",
                   (lambda E, dst_ap=dst_ap, tview=tview:
                    E.tensor_copy(dst_ap, tview)),
                   wait=[("mm_c", m)], sem="ve_c", inc=1)
        return v

    ndir = 2 if MAMBA_PHASE >= 6 else 1
    for d in range(ndir):
        xs_d = xs_d_in[d]
        lb.dma("sync", VEC[0:32, 0:1], dtb_in[d][:, :], sem="ld_v",
               wait=[("ve_c", lb.at("ve_c")), ("ac_c", lb.at("ac_c")),
                     ("mm_c", lb.at("mm_c"))])
        lb.dma("sync", VEC[0:32, 1:2], alog_in[d][:, :], sem="ld_v")
        lb.dma("sync", VEC[:, 2:18], rmsw_in[d][:, :], sem="ld_v")
        lb.dma("sync", VEC[0:1, 18:50], drow_in[d][:, :], sem="ld_v")
        lb.dma("sync", CWS[:, :], cw_in[d][:, :], sem="ld_v")
        lb.dma("sync", CBS[:, :], cb_in[d][:, :], sem="ld_v")
        ld_v = lb.at("ld_v")
        a1 = lb.raw("scalar", lambda E: E.activation(ACOL[:, :], VEC[0:32, 1:2],
                                                     AF.Exp),
                    wait=[("ld_v", ld_v), ("ve_i", v_init)], sem="ac_c", inc=1)
        lb.raw("vector", lambda E: E.tensor_scalar_mul(ACOL[:, :], ACOL[:, :],
                                                       -1.0),
               wait=[("ac_c", a1)], sem="ve_c", inc=1)
        mmd = lb.raw("tensor",
                     (lambda E: E.matmul(pb[4][:, 0:32], BCO[0:1, :],
                                         VEC[0:1, 18:50], start=True, stop=True,
                                         skip_group_check=True)),
                     wait=[("ld_v", ld_v), ("ve_c", lb.at("ve_c")),
                           ("ac_c", lb.at("ac_c"))],
                     sem="mm_c", inc=1)
        lb.raw("vector", lambda E: E.tensor_copy(DCOL[:, :], pb[4][:, 0:32]),
               wait=[("mm_c", mmd)], sem="ve_c", inc=1)
        lb.raw("vector", lambda E: E.memset(S_[:, :], 0.0), sem="ve_c", inc=1)
        lb.raw("vector", lambda E: E.memset(CARRY[:, :], 0.0), sem="ve_c", inc=1)
        lb.raw("vector", lambda E: E.memset(U[:, :], 0.0),
               wait=[("mm_p", lb.at("mm_p")), ("mm_c", lb.at("mm_c"))],
               sem="ve_c", inc=1)
        for c in range(16):
            ldw = lb.dma("gpsimd", AW[:, 0:1024],
                         opT[d][c * 128:(c + 1) * 128, :], sem="ldg",
                         wait=[("ve_c", lb.at("ve_c")), ("mm_p", lb.at("mm_p")),
                               ("mm_o", lb.at("mm_o")), ("mm_z", lb.at("mm_z")),
                               ("mm_f", lb.at("mm_f")),
                               ("st_op", lb.at("st_op"))])
            vsc = lb.raw("vector",
                         (lambda E, c=c: E.tensor_scalar_mul(
                             AW[:, 0:1024], AW[:, 0:1024], VEC[:, 2 + c:3 + c])),
                         wait=[("ldg", ldw)], sem="ve_c", inc=1)
            lb.dma("sync", ops_d[d][c], AW[:, 0:1024], sem="st_op",
                   wait=[("ve_c", vsc)])
        v_pre = lb.at("ve_c")
        ld_pre = lb.at("st_op")
        lb.wbuf_free[0] = ("st_op", ld_pre)
        lb.wbuf_free[1] = ("st_op", ld_pre)

        nblk_run = NBLK if MAMBA_PHASE >= 4 else (6 if MAMBA_PHASE >= 3 else 1)
        for blk in range(nblk_run):
            own = blk >= NBLK - OWNB
            t0 = blk * MB_
            for k in range(KCH):
                lb.dma("sync", A0[:, k * MB_:(k + 1) * MB_],
                       xs_d[k * 128:(k + 1) * 128, t0:t0 + MB_],
                       sem="ld_x",
                       wait=([("ve_lnm", lb.at("ve_lnm")),
                              ("ve_c", lb.at("ve_c")),
                              ("mm_o", lb.at("mm_o"))] if k == 0 else None))
            ld_blk = lb.at("ld_x")
            ve_ln, _ = emit_ln3(lb, "lnm", A0, XN, SCR, STATS, ONES, BCO,
                                pb[0:4],
                                wait_x=[("ld_x", ld_blk), ("ve_i", v_init),
                                        ("ve_c", lb.at("ve_c"))],
                                eps_ap=EPS[0:1, 0:1], Tn=MB_, stats_bufs=1)

            def evac_u(m, tt, bank, mmval):
                a = lb.raw("vector",
                           (lambda E, m=m, tt=tt, bank=bank: E.tensor_copy(
                               U[:, m * (MB_ + 4) + 4 + tt * 512:
                                 m * (MB_ + 4) + 4 + tt * 512 + 512],
                               bank[:, :])),
                           wait=[("mm_p", mmval), ("mm_c", lb.at("mm_c")),
                                 ("ac_c", lb.at("ac_c"))],
                           sem="ve_c", inc=1)
                return ("ve_c", a)

            emit_proj(lb, "p", ipT[d], AW, XN, D, 2432, MB_, evac_u, pb[0:4],
                      m_panel=2, wait_rhs=[("ve_lnm", ve_ln)])
            ve_u = lb.at("ve_c")
            if MAMBA_PHASE < 2:
                continue

            dt0 = 18 * (MB_ + 4) + 4
            v_dt0 = lb.raw("vector",
                           (lambda E: E.tensor_scalar_add(
                               DTL[:, :], U[0:32, dt0:dt0 + MB_],
                               VEC[0:32, 0:1])),
                           wait=[("ve_c", ve_u)], sem="ve_c", inc=1)
            a_ab = lb.raw("scalar",
                          (lambda E: E.activation(L1_[:, :], DTL[:, :], AF.Abs)),
                          wait=[("ve_c", v_dt0)], sem="ac_c", inc=1)
            lb.raw("scalar",
                   (lambda E: E.activation(L1_[:, :], L1_[:, :], AF.Exp,
                                           scale=-1.0)),
                   sem="ac_c", inc=1)
            a_ln = lb.raw("scalar",
                          (lambda E: E.activation(L1_[:, :], L1_[:, :], AF.Ln,
                                                  bias=ONE32[0:32, 0:1])),
                          sem="ac_c", inc=1)
            lb.raw("vector",
                   (lambda E: E.tensor_scalar_max(DTL[:, :], DTL[:, :], 0.0)),
                   wait=[("ac_c", a_ab)], sem="ve_c", inc=1)
            v_dt = lb.raw("vector",
                          (lambda E: E.tensor_add(DTL[:, :], DTL[:, :],
                                                  L1_[:, :])),
                          wait=[("ac_c", a_ln)], sem="ve_c", inc=1)
            v_al = lb.raw("vector",
                          (lambda E: E.tensor_scalar_mul(L1_[:, :], DTL[:, :],
                                                         ACOL[0:32, 0:1])),
                          sem="ve_c", inc=1)
            v_cm = lb.raw("vector",
                          (lambda E: E.tensor_tensor_scan(
                              CUM[:, 1:MB_ + 1], LONE[:, :], L1_[:, :],
                              CARRY[0:32, 0:1], ALU.mult, ALU.add)),
                          sem="ve_c", inc=1)
            lb.raw("vector",
                   (lambda E: E.tensor_copy(CUM[:, 0:1], CARRY[0:32, 0:1])),
                   sem="ve_c", inc=1)
            lb.raw("vector",
                   (lambda E: E.tensor_copy(CARRY[0:32, 0:1],
                                            CUM[:, MB_:MB_ + 1])),
                   sem="ve_c", inc=1)

            for c in range(18):
                cb0 = c * (MB_ + 4)
                v_st = lb.raw("vector",
                              (lambda E, c=c, cb0=cb0: E.tensor_copy(
                                  STASH[:, c * 3:(c + 1) * 3],
                                  U[:, cb0 + 1 + MB_:cb0 + 4 + MB_])),
                              wait=[("ve_c", ve_u)], sem="ve_c", inc=1)
                for j in range(4):
                    src = U[:, cb0 + 1 + j: cb0 + 1 + j + MB_]
                    if j == 0:
                        lb.raw("vector",
                               (lambda E, c=c, src=src: E.tensor_scalar_mul(
                                   SCR[:, 0:MB_], src,
                                   CWS[:, c * 4:c * 4 + 1])),
                               sem="ve_c", inc=1)
                    else:
                        lb.raw("vector",
                               (lambda E, c=c, j=j, src=src:
                                E.scalar_tensor_tensor(
                                    SCR[:, 0:MB_], src,
                                    CWS[:, c * 4 + j:c * 4 + j + 1],
                                    SCR[:, 0:MB_], ALU.mult, ALU.add)),
                               sem="ve_c", inc=1)
                v_cv = lb.at("ve_c")
                a_si = lb.raw("scalar",
                              (lambda E, c=c, cb0=cb0: E.activation(
                                  U[:, cb0 + 4:cb0 + 4 + MB_], SCR[:, 0:MB_],
                                  AF.Silu, bias=CBS[:, c:c + 1])),
                              wait=[("ve_c", v_cv)], sem="ac_c", inc=1)
                lb.raw("vector",
                       (lambda E, c=c, cb0=cb0: E.tensor_copy(
                           U[:, cb0 + 1:cb0 + 4],
                           STASH[:, c * 3:(c + 1) * 3])),
                       wait=[("ac_c", a_si)], sem="ve_c", inc=1)
            ve_cv = lb.at("ve_c")
            ac_cv = lb.at("ac_c")

            if MAMBA_PHASE < 3:
                continue
            nch = MB_ // 128
            if not own:
                a_w = lb.raw("scalar",
                             (lambda E: E.activation(WL[:, :], CUM[:, 1:MB_ + 1],
                                                     AF.Exp, scale=-1.0,
                                                     bias=CUM[0:32, MB_:MB_ + 1])),
                             wait=[("ve_c", v_cm)], sem="ac_c", inc=1)
                v_w = lb.raw("vector",
                             (lambda E: E.tensor_mul(WL[:, :], WL[:, :],
                                                     DTL[:, :])),
                             wait=[("ac_c", a_w)], sem="ve_c", inc=1)
                ldm = lb.dma("sync", MSKB[:, :], msk_in[d][0:1, t0:t0 + MB_],
                             sem="ld_m", wait=[("mm_c", lb.at("mm_c"))])
                for half in range(2):
                    mmk = lb.raw("tensor",
                                 (lambda E, half=half: E.matmul(
                                     pb[4][0:32, :], ONESRF[0:1, 0:32],
                                     MSKB[0:1, half * 512:half * 512 + 512],
                                     start=True, stop=True,
                                     skip_group_check=True)),
                                 wait=[("ve_c", v_w), ("ld_m", ldm)],
                                 sem="mm_c", inc=1)
                    v_w = lb.raw("vector",
                                 (lambda E, half=half: E.tensor_tensor(
                                     WL[:, half * 512:(half + 1) * 512],
                                     WL[:, half * 512:(half + 1) * 512],
                                     pb[4][0:32, :], ALU.mult)),
                                 wait=[("mm_c", mmk)], sem="ve_c", inc=1)
                for kc in range(nch):
                    if MAMBA_SYNC >= 3:
                        lb.join(("ve_c", "ac_c", "mm_c", "mm_z"))
                    vw = transpose_to(WT32[:, :], WL[0:32, kc * 128:(kc + 1) * 128],
                                      ID32[:, :], 32, 128, True)
                    vb = transpose_to(BT[:, :],
                                      U[0:64, 16 * (MB_ + 4) + 4 + kc * 128:
                                        16 * (MB_ + 4) + 4 + (kc + 1) * 128],
                                      ID128[0:64, 0:64], 64, 128, False)
                    for c in range(16):
                        vx = transpose_to(XT2[:, :],
                                          U[:, c * (MB_ + 4) + 4 + kc * 128:
                                            c * (MB_ + 4) + 4 + (kc + 1) * 128],
                                          ID128[:, :], 128, 128, False)
                        v_xw = lb.raw("vector",
                                      (lambda E, c=c: E.tensor_tensor(
                                          XW[:, :].rearrange("p (h q) -> p h q", h=2),
                                          XT2[:, :].rearrange("p (h q) -> p h q", h=2),
                                          WT32[:, 2 * c:2 * c + 2].rearrange(
                                              "p h -> p h 1" if False else "p (h o) -> p h o", h=2
                                          ).to_broadcast([128, 2, 64]),
                                          ALU.mult)),
                                      wait=[("ve_c", vx)], sem="ve_c", inc=1)
                        for hh in range(2):
                            h = 2 * c + hh
                            g = h // 8
                            col = (h % 8) * 64
                            st = (kc == 0 and h % 8 == 0)
                            lb.raw("tensor",
                                   (lambda E, g=g, col=col, hh=hh, st=st:
                                    E.matmul(pb[g][0:64, col:col + 64],
                                             BT[:, :],
                                             XW[:, hh * 64:(hh + 1) * 64],
                                             start=st, stop=False,
                                             skip_group_check=True)),
                                   wait=[("ve_c", v_xw), ("ve_c", vb)]
                                   if (c == 0 and hh == 0) else
                                   [("ve_c", v_xw)],
                                   sem="mm_c", inc=1)
                mm_ct = lb.at("mm_c")
                v_lc = lb.raw("vector",
                              (lambda E: E.tensor_sub(COL1[:, :],
                                                      CUM[:, MB_:MB_ + 1],
                                                      CUM[:, 0:1])),
                              sem="ve_c", inc=1)
                a_le = lb.raw("scalar",
                              (lambda E: E.activation(COL1[:, :], COL1[:, :],
                                                      AF.Exp)),
                              wait=[("ve_c", v_lc)], sem="ac_c", inc=1)
                v_r32 = transpose_to(ROW32[:, :], COL1[:, :], ID32[:, :],
                                     32, 1, True)
                for g in range(4):
                    v_er = lb.raw("vector",
                                  (lambda E, g=g: E.tensor_copy(
                                      EROW[0:1, :].rearrange("o (h q) -> o h q", h=8),
                                      ROW32[0:1, 8 * g:8 * g + 8].rearrange(
                                          "o (h z) -> o h z", h=8
                                      ).to_broadcast([1, 8, 64]))),
                                  wait=[("ve_c", v_r32), ("ac_c", a_le),
                                        ("mm_c", lb.at("mm_c"))],
                                  sem="ve_c", inc=1)
                    mm_e = lb.raw("tensor",
                                  (lambda E: E.matmul(pb[4][0:64, :],
                                                      ONESRF[0:1, 0:64],
                                                      EROW[0:1, :],
                                                      start=True, stop=True,
                                                      skip_group_check=True)),
                                  wait=[("ve_c", v_er)], sem="mm_c", inc=1)
                    lb.raw("vector",
                           (lambda E, g=g: E.tensor_tensor(
                               S_[:, g * 512:(g + 1) * 512],
                               S_[:, g * 512:(g + 1) * 512],
                               pb[4][0:64, :], ALU.mult)),
                           wait=[("mm_c", mm_e)], sem="ve_c", inc=1)
                    v_su = lb.raw("vector",
                                  (lambda E, g=g: E.tensor_tensor(
                                      S_[:, g * 512:(g + 1) * 512],
                                      S_[:, g * 512:(g + 1) * 512],
                                      pb[g][0:64, :], ALU.add)),
                                  wait=[("mm_c", mm_ct)], sem="ve_c", inc=1)
                    lb.bank_free[pb[g].name] = ("ve_c", v_su)
            else:
                ob = blk - (NBLK - OWNB)
                for kc in range(MB_ // 128):
                    if MAMBA_SYNC >= 1:
                        lb.join(("ve_c", "ac_c", "mm_c", "mm_z"))
                    tok0 = ob * MB_ + kc * 128
                    c16 = 16 * (MB_ + 4) + 4 + kc * 128
                    c17 = 17 * (MB_ + 4) + 4 + kc * 128
                    cums = CUM[:, 1 + kc * 128:1 + (kc + 1) * 128]
                    kend = CUM[0:32, (kc + 1) * 128:(kc + 1) * 128 + 1]
                    kstart = CUM[0:32, kc * 128:kc * 128 + 1]
                    dts = DTL[:, kc * 128:(kc + 1) * 128]
                    a_w = lb.raw("scalar",
                                 (lambda E, cums=cums, kend=kend: E.activation(
                                     WL[:, 0:128], cums, AF.Exp, scale=-1.0,
                                     bias=kend)),
                                 wait=[("ve_c", lb.at("ve_c"))],
                                 sem="ac_c", inc=1)
                    v_w = lb.raw("vector",
                                 (lambda E, dts=dts: E.tensor_mul(
                                     WL[:, 0:128], WL[:, 0:128], dts)),
                                 wait=[("ac_c", a_w)], sem="ve_c", inc=1)
                    v_nk = lb.raw("vector",
                                  (lambda E, kstart=kstart: E.tensor_scalar_mul(
                                      COL1[:, :], kstart, -1.0)),
                                  sem="ve_c", inc=1)
                    a_e1 = lb.raw("scalar",
                                  (lambda E, cums=cums: E.activation(
                                      L1_[:, 0:128], cums, AF.Exp,
                                      bias=COL1[0:32, 0:1])),
                                  wait=[("ve_c", v_nk)], sem="ac_c", inc=1)
                    transpose_to(WT32[:, :], WL[0:32, 0:128], ID32[:, :],
                                 32, 128, True)
                    transpose_to(CT32[:, :], cums, ID32[:, :], 32, 128, True)
                    transpose_to(DTT32[:, :], dts, ID32[:, :], 32, 128, True)
                    v_tr = transpose_to(E1T32[:, :], L1_[0:32, 0:128],
                                        ID32[:, :], 32, 128, True)
                    mm_g = lb.raw("tensor",
                                  (lambda E, c16=c16, c17=c17: E.matmul(
                                      pb[4][:, 0:128], U[0:64, c16:c16 + 128],
                                      U[0:64, c17:c17 + 128],
                                      start=True, stop=True,
                                      skip_group_check=True)),
                                  wait=[("ve_c", lb.at("ve_c")),
                                        ("ac_c", ac_cv)],
                                  sem="mm_c", inc=1)
                    v_g = lb.raw("vector",
                                 (lambda E: E.tensor_copy(GSB[:, :],
                                                          pb[4][:, 0:128])),
                                 wait=[("mm_c", mm_g)], sem="ve_c", inc=1)
                    v_sb = lb.raw("vector",
                                  (lambda E: E.tensor_copy(SB_[:, :], S_[:, :])),
                                  wait=[("mm_c", lb.at("mm_c"))],
                                  sem="ve_c", inc=1)
                    for g in range(8):
                        ct4 = CT32[:, 4 * g:4 * g + 4]
                        v_dg = lb.raw("vector",
                                      (lambda E, ct4=ct4: E.tensor_tensor(
                                          DGA[:, :].rearrange("p (h t) -> p h t", h=4),
                                          ID128[:, :].rearrange("p (o t) -> p o t", o=1
                                          ).to_broadcast([128, 4, 128]),
                                          ct4.rearrange("p (h o) -> p h o", h=4
                                          ).to_broadcast([128, 4, 128]),
                                          ALU.mult)),
                                      wait=[("mm_c", lb.at("mm_c")),
                                            ("ve_c", v_g)],
                                      sem="ve_c", inc=1)
                        mm_e = lb.raw("tensor",
                                      (lambda E: E.matmul(
                                          pb[4][:, :], ONE128F[:, :], DGA[:, :],
                                          start=True, stop=True,
                                          skip_group_check=True)),
                                      wait=[("ve_c", v_dg)], sem="mm_c", inc=1)
                        v_mm = lb.raw("vector",
                                      (lambda E, ct4=ct4: E.tensor_tensor(
                                          M0M[:, :].rearrange("p (h t) -> p h t", h=4),
                                          MTRI[:, :].rearrange("p (o t) -> p o t", o=1
                                          ).to_broadcast([128, 4, 128]),
                                          ct4.rearrange("p (h o) -> p h o", h=4
                                          ).to_broadcast([128, 4, 128]),
                                          ALU.subtract)),
                                      sem="ve_c", inc=1)
                        v_dl = lb.raw("vector",
                                      (lambda E: E.tensor_add(DGA[:, :],
                                                              pb[4][:, :],
                                                              M0M[:, :])),
                                      wait=[("mm_c", mm_e), ("ve_c", v_mm)],
                                      sem="ve_c", inc=1)
                        a_ex = lb.raw("scalar",
                                      (lambda E: E.activation(ESB[:, :],
                                                              DGA[:, :],
                                                              AF.Exp)),
                                      wait=[("ve_c", v_dl)], sem="ac_c", inc=1)
                        dt4 = DTT32[:, 4 * g:4 * g + 4]
                        v_w1 = lb.raw("vector",
                                      (lambda E, dt4=dt4: E.tensor_tensor(
                                          ESB[:, :].rearrange("p (h t) -> p h t", h=4),
                                          ESB[:, :].rearrange("p (h t) -> p h t", h=4),
                                          dt4.rearrange("p (h o) -> p h o", h=4
                                          ).to_broadcast([128, 4, 128]),
                                          ALU.mult)),
                                      wait=[("ac_c", a_ex)], sem="ve_c", inc=1)
                        lb.raw("vector",
                               (lambda E, g=g: E.tensor_tensor(
                                   MSB[:, g * 512:(g + 1) * 512].rearrange(
                                       "p (h t) -> p h t", h=4),
                                   ESB[:, :].rearrange("p (h t) -> p h t", h=4),
                                   GSB[:, :].rearrange("p (o t) -> p o t", o=1
                                   ).to_broadcast([128, 4, 128]),
                                   ALU.mult)),
                               sem="ve_c", inc=1)
                    ve_msb = lb.at("ve_c")
                    for fh in range(2):
                        for e in range(KCH):
                            lb.dma("gpsimd", AW[:, 4096 + e * 1024:
                                                 4096 + (e + 1) * 1024],
                                   wzT[d][e * 128:(e + 1) * 128,
                                          fh * 1024:(fh + 1) * 1024],
                                   sem="ldg",
                                   wait=[("mm_z", lb.at("mm_z"))]
                                   if e == 0 else None)
                        ldz = lb.at("ldg")
                        for ft in range(2):
                            for e in range(KCH):
                                w0 = []
                                if e == 0:
                                    fr = lb.bank_free.pop(pb[7].name, None)
                                    if fr:
                                        w0.append(fr)
                                    w0.append(("ldg", ldz))
                                    if ft == 0 and fh == 0:
                                        w0.append(("ve_lnm", ve_ln))
                                lb.raw("tensor",
                                       (lambda E, ft=ft, e=e, kc=kc: E.matmul(
                                           pb[7][:, :],
                                           XN[:, e * MB_ + kc * 128:
                                              e * MB_ + (kc + 1) * 128],
                                           AW[:, 4096 + e * 1024 + ft * 512:
                                              4096 + e * 1024 + ft * 512 + 512],
                                           start=(e == 0), stop=(e == KCH - 1),
                                           skip_group_check=True)),
                                       wait=w0 or None,
                                       sem="mm_z" if e == KCH - 1 else None,
                                       inc=1 if e == KCH - 1 else None)
                            mmz = lb.at("mm_z")
                            a_z = lb.raw("scalar",
                                         (lambda E, ft=ft, fh=fh: E.activation(
                                             ZS[:, fh * 1024 + ft * 512:
                                                fh * 1024 + ft * 512 + 512],
                                             pb[7][:, :], AF.Silu)),
                                         wait=[("mm_z", mmz)], sem="ac_c", inc=1)
                            lb.bank_free[pb[7].name] = ("ac_c", a_z)
                    ac_z = lb.at("ac_c")
                    if MAMBA_SYNC >= 2:
                        lb.join(("ve_c", "ac_c", "mm_c", "mm_z"))
                    for c in range(16):
                        if MAMBA_SYNC >= 3:
                            lb.join(("ve_c", "ac_c", "mm_c", "mm_z"))
                        vx = transpose_to(XT2[:, :],
                                          U[:, c * (MB_ + 4) + 4 + kc * 128:
                                            c * (MB_ + 4) + 4 + (kc + 1) * 128],
                                          ID128[:, :], 128, 128, False)
                        v_xw = lb.raw("vector",
                                      (lambda E, c=c: E.tensor_tensor(
                                          XW[:, :].rearrange("p (h q) -> p h q", h=2),
                                          XT2[:, :].rearrange("p (h q) -> p h q", h=2),
                                          WT32[:, 2 * c:2 * c + 2].rearrange(
                                              "p (h o) -> p h o", h=2
                                          ).to_broadcast([128, 2, 64]),
                                          ALU.mult)),
                                      wait=[("ve_c", vx)], sem="ve_c", inc=1)
                        if c == 0:
                            vb = transpose_to(BT[:, :], U[0:64, c16:c16 + 128],
                                              ID128[0:64, 0:64], 64, 128, False)
                        for hh in range(2):
                            h = 2 * c + hh
                            g = h // 8
                            col = (h % 8) * 64
                            bank6 = pb[6]
                            m_y1 = lb.raw("tensor",
                                          (lambda E, c17=c17, h=h: E.matmul(
                                              bank6[:, 0:64],
                                              U[0:64, c17:c17 + 128],
                                              SB_[:, h * 64:(h + 1) * 64],
                                              start=True, stop=True,
                                              skip_group_check=True)),
                                          wait=[("ve_c", lb.at("ve_c")),
                                                ("ve_c", v_sb)],
                                          sem="mm_c", inc=1)
                            v_y1 = lb.raw("vector",
                                          (lambda E, h=h: E.tensor_scalar_mul(
                                              Y1S[:, :], bank6[:, 0:64],
                                              E1T32[:, h:h + 1])),
                                          wait=[("mm_c", m_y1),
                                                ("ve_c", v_tr)],
                                          sem="ve_c", inc=1)
                            m_y2 = lb.raw("tensor",
                                          (lambda E, h=h, hh=hh: E.matmul(
                                              bank6[:, 0:64],
                                              MSB[:, h * 128:(h + 1) * 128],
                                              XT2[:, hh * 64:(hh + 1) * 64],
                                              start=True, stop=True,
                                              skip_group_check=True)),
                                          wait=[("ve_c", v_y1),
                                                ("ve_c", ve_msb)],
                                          sem="mm_c", inc=1)
                            v_c1 = lb.raw("vector",
                                          (lambda E, h=h, hh=hh: E.scalar_tensor_tensor(
                                              YTT[:, h * 64:(h + 1) * 64],
                                              XT2[:, hh * 64:(hh + 1) * 64],
                                              DCOL[:, h:h + 1], Y1S[:, :],
                                              ALU.mult, ALU.add)),
                                          sem="ve_c", inc=1)
                            v_c2 = lb.raw("vector",
                                          (lambda E, h=h: E.tensor_tensor(
                                              YTT[:, h * 64:(h + 1) * 64],
                                              YTT[:, h * 64:(h + 1) * 64],
                                              bank6[:, 0:64], ALU.add)),
                                          wait=[("mm_c", m_y2)],
                                          sem="ve_c", inc=1)
                            fr = lb.bank_free.pop(pb[g].name, None)
                            w1 = [("ve_c", v_xw)]
                            if hh == 0 and c % 8 == 0:
                                w1.append(("ve_c", vb))
                            if fr:
                                w1.append(fr)
                            lb.raw("tensor",
                                   (lambda E, g=g, col=col, hh=hh, h=h:
                                    E.matmul(pb[g][0:64, col:col + 64],
                                             BT[:, :],
                                             XW[:, hh * 64:(hh + 1) * 64],
                                             start=(col == 0), stop=False,
                                             skip_group_check=True)),
                                   wait=w1, sem="mm_c", inc=1)
                    mm_ct = lb.at("mm_c")
                    if MAMBA_SYNC >= 2:
                        lb.join(("ve_c", "ac_c", "mm_c", "mm_z"))
                    v_yz = lb.raw("vector",
                                  (lambda E: E.tensor_tensor(
                                      YTT[:, :], YTT[:, :], ZS[:, :], ALU.mult)),
                                  wait=[("ac_c", ac_z),
                                        ("ve_c", lb.at("ve_c"))],
                                  sem="ve_c", inc=1)
                    v_sq = lb.raw("vector",
                                  (lambda E: E.scalar_tensor_tensor(
                                      SCRF[:, 0:2048], YTT[:, :], 1.0,
                                      YTT[:, :], ALU.mult, ALU.mult,
                                      accum_out=SQC[:, :])),
                                  wait=[("ve_c", v_yz)], sem="ve_c", inc=1)
                    a_r1 = lb.raw("scalar",
                                  (lambda E: E.activation(SQC[:, :], SQC[:, :],
                                                          AF.Ln,
                                                          scale=1.0 / 2048.0,
                                                          bias=EPS128[:, 0:1])),
                                  wait=[("ve_c", v_sq)], sem="ac_c", inc=1)
                    a_r2 = lb.raw("scalar",
                                  (lambda E: E.activation(SQC[:, :], SQC[:, :],
                                                          AF.Exp, scale=-0.5)),
                                  sem="ac_c", inc=1)
                    v_ys = lb.raw("vector",
                                  (lambda E: E.tensor_scalar_mul(
                                      YTT[:, :], YTT[:, :], SQC[:, 0:1])),
                                  wait=[("ac_c", a_r2)], sem="ve_c", inc=1)
                    v_yc = lb.raw("vector",
                                  (lambda E: E.tensor_copy(YFMC[:, :],
                                                           YTT[:, :])),
                                  wait=[("st_of", lb.at("st_of"))],
                                  sem="ve_c", inc=1)
                    lb.raw("vector", lambda E: E.engine_nop(),
                           wait=[("st_f", lb.at("st_f"))], sem="ve_c", inc=1)
                    for c in range(16):
                        transpose_to(YFM[:, c * 128:(c + 1) * 128],
                                     YFMC[:, c * 128:(c + 1) * 128],
                                     ID128[:, :], 128, 128, False)
                    v_yf = lb.at("ve_c")
                    lb.dma("sync",
                           yfm_d[d][:, :, tok0:tok0 + 128].rearrange(
                               "c p t -> p c t"),
                           YFM[:, :].rearrange("p (c t) -> p c t", c=16),
                           sem="st_f", wait=[("ve_c", v_yf)])
                    v_lc = lb.raw("vector",
                                  (lambda E, kend=kend, kstart=kstart:
                                   E.tensor_sub(COL1[:, :], kend, kstart)),
                                  sem="ve_c", inc=1)
                    a_le = lb.raw("scalar",
                                  (lambda E: E.activation(COL1[:, :],
                                                          COL1[:, :], AF.Exp)),
                                  wait=[("ve_c", v_lc)], sem="ac_c", inc=1)
                    v_r32 = transpose_to(ROW32[:, :], COL1[:, :], ID32[:, :],
                                         32, 1, True)
                    for g in range(4):
                        v_er = lb.raw("vector",
                                      (lambda E, g=g: E.tensor_copy(
                                          EROW[0:1, :].rearrange(
                                              "o (h q) -> o h q", h=8),
                                          ROW32[0:1, 8 * g:8 * g + 8].rearrange(
                                              "o (h z) -> o h z", h=8
                                          ).to_broadcast([1, 8, 64]))),
                                      wait=[("ve_c", v_r32), ("ac_c", a_le),
                                            ("mm_c", lb.at("mm_c"))],
                                      sem="ve_c", inc=1)
                        mm_e2 = lb.raw("tensor",
                                       (lambda E: E.matmul(pb[4][0:64, :],
                                                           ONESRF[0:1, 0:64],
                                                           EROW[0:1, :],
                                                           start=True, stop=True,
                                                           skip_group_check=True)),
                                       wait=[("ve_c", v_er)], sem="mm_c", inc=1)
                        lb.raw("vector",
                               (lambda E, g=g: E.tensor_tensor(
                                   S_[:, g * 512:(g + 1) * 512],
                                   S_[:, g * 512:(g + 1) * 512],
                                   pb[4][0:64, :], ALU.mult)),
                               wait=[("mm_c", mm_e2)], sem="ve_c", inc=1)
                        v_su = lb.raw("vector",
                                      (lambda E, g=g: E.tensor_tensor(
                                          S_[:, g * 512:(g + 1) * 512],
                                          S_[:, g * 512:(g + 1) * 512],
                                          pb[g][0:64, :],
                                          ALU.add)),
                                      wait=[("mm_c", mm_ct)], sem="ve_c", inc=1)
                        lb.bank_free[pb[g].name] = ("ve_c", v_su)

        if MAMBA_PHASE < 5:
            continue
        opw = ops_d[d].rearrange("c p m -> (c p) m")
        for th in range(2):
            for c in range(16):
                lb.dma("sync", A0b[:, c * 1024:(c + 1) * 1024],
                       yfm_d[d][c, :, th * 1024:(th + 1) * 1024],
                       sem="ld_x2",
                       wait=([("ve_lnm", lb.at("ve_lnm")),
                              ("ve_c", lb.at("ve_c")),
                              ("mm_c", lb.at("mm_c")),
                              ("mm_o", lb.at("mm_o")),
                              ("st_f", lb.at("st_f"))] if c == 0 else None))
            ld_y = lb.at("ld_x2")

            def evac_of(m, tt, bank, mmval, th=th, d=d):
                v1 = lb.raw("vector",
                            (lambda E, bank=bank: E.tensor_copy(
                                YFMC[:, 0:512], bank[:, :])),
                            wait=[("mm_o", mmval),
                                  ("st_of", lb.at("st_of"))],
                            sem="ve_c", inc=1)
                lb.dma("sync",
                       of_d[d][m, :, th * 1024 + tt * 512:
                               th * 1024 + tt * 512 + 512],
                       YFMC[:, 0:512], sem="st_of", wait=[("ve_c", v1)])
                return ("ve_c", v1)

            emit_proj(lb, "o", opw, AW, A0b, 2048, D, 1024, evac_of,
                      pb[0:4], rhs_stride=1024, m_panel=2,
                      wait_rhs=[("ld_x2", ld_y)],
                      wait_w=[("st_op", ld_pre)])

    if MAMBA_PHASE < 6.5:
        pass
    LNW = lb.sb("LNW", [128, KCH], F32)
    LNB = lb.sb("LNB", [128, KCH], F32)
    PBI = lb.sb("PBI", [128, KCH], F32)
    lb.dma("sync", LNW[:, :], lnw[:, :], sem="ld")
    lb.dma("sync", LNB[:, :], lnb[:, :], sem="ld")
    lb.dma("sync", PBI[:, :], pbias[:, :], sem="ld")
    ld_f = lb.at("ld")
    if MAMBA_PHASE < 11:
        DUMMY_TAIL = True
    if False:
        for k in range(KCH):
            lb.dma("sync", y_out[k * 128:(k + 1) * 128, :],
                   A0[:, 0:T],
                   sem="st_y",
                   wait=[("ve_c", lb.at("ve_c")), ("ac_c", lb.at("ac_c")),
                         ("mm_c", lb.at("mm_c")), ("mm_p", lb.at("mm_p")),
                         ("ve_lnm", lb.at("ve_lnm")),
                         ("ld_x", lb.at("ld_x")), ("st_of", lb.at("st_of")),
                         ("st_f", lb.at("st_f")), ("ldg", lb.at("ldg")),
                         ("ld_x2", lb.at("ld_x2")),
                         ("mm_z", lb.at("mm_z")),
                         ("mm_o", lb.at("mm_o"))] if k == 0 else None)
        lb.raw("sync", lambda E: E.nop(), wait=[("st_y", lb.at("st_y"))])
        return lb.finalize()
    for th in range(2):
        for c in range(KCH):
            lb.dma("sync", U[:, c * 1024:(c + 1) * 1024],
                   of_d[0][c, :, th * 1024:(th + 1) * 1024],
                   sem="ld_x2",
                   wait=([("mm_p", lb.at("mm_p")), ("ve_c", lb.at("ve_c")),
                          ("st_of", lb.at("st_of")),
                          ("mm_f", lb.at("mm_f")),
                          ("ve_lnf", lb.at("ve_lnf")),
                          ("ve_lnh", lb.at("ve_lnh")),
                          ("st_y", lb.at("st_y"))] if c == 0 else None))
        for c in range(KCH):
            lb.dma("sync", SCR[:, c * 1024:(c + 1) * 1024],
                   of_d[1][c, :, (1 - th) * 1024:(2 - th) * 1024],
                   sem="ld_x2")
        ld_of = lb.at("ld_x2")
        v_rev = None
        for c in range(KCH):
            v_rev = lb.raw("vector",
                           (lambda E, c=c: E.tensor_copy(
                               U[:, (KCH + c) * 1024:(KCH + c + 1) * 1024],
                               SCR[:, c * 1024:(c + 1) * 1024][:, ::-1])),
                           wait=[("ld_x2", ld_of)] if c == 0 else None,
                           sem="ve_c", inc=1)
        if MAMBA_PHASE < 8:
            continue
        for k in range(KCH):
            lb.dma("sync", A0[:, k * 1024:(k + 1) * 1024],
                   xs_d_in[0][k * 128:(k + 1) * 128,
                              6144 + th * 1024:6144 + (th + 1) * 1024],
                   sem="ld_x2", wait=[("ve_c", v_rev)] if k == 0 else None)
        ld_xh = lb.at("ld_x2")
        ve_lnf, _ = emit_ln3(lb, "lnf", A0, XN, SCR, STATS, ONES, BCO,
                             pb[0:4],
                             wait_x=[("ld_x2", ld_xh), ("ve_i", v_init),
                                     ("ve_c", lb.at("ve_c"))],
                             eps_ap=EPS[0:1, 0:1], Tn=1024, stats_bufs=1)

        def evac_h(m, tt, bank, mmval, th=th):
            v1 = lb.raw("vector",
                        (lambda E, m=m, tt=tt, bank=bank: E.tensor_scalar(
                            YT[:, :], bank[:, :], 1.0, PBI[:, m:m + 1],
                            ALU.mult, ALU.add)),
                        wait=[("mm_f", mmval), ("ld", ld_f),
                              ("ve_lnf", ve_lnf)],
                        sem="ve_c", inc=1)
            v2 = lb.raw("vector",
                        (lambda E, m=m, tt=tt: E.tensor_tensor(
                            A0[:, m * 1024 + tt * 512:m * 1024 + tt * 512 + 512],
                            YT[:, :],
                            XN[:, m * 1024 + tt * 512:m * 1024 + tt * 512 + 512],
                            ALU.add)),
                        sem="ve_c", inc=1)
            return ("ve_c", v2)

        if MAMBA_PHASE < 9:
            continue
        emit_proj(lb, "f", pwT, AW, U, 2048, D, 1024, evac_h, pb[0:4],
                  rhs_stride=1024, m_panel=2,
                  wait_rhs=[("ve_c", lb.at("ve_c")), ("ve_lnf", ve_lnf)])
        ve_h = lb.at("ve_c")
        if MAMBA_PHASE < 10:
            continue
        ve_lnh, _ = emit_ln3(lb, "lnh", A0, XN, SCR, STATS, ONES, BCO,
                             pb[0:4],
                             wait_x=[("ve_c", ve_h)],
                             eps_ap=EPS[0:1, 0:1], Tn=1024,
                             gbias=(LNW, LNB), stats_bufs=1)
        if MAMBA_PHASE < 11:
            continue
        for m in range(KCH):
            for tt in range(2):
                ldx = lb.dma("sync", XT[:, :],
                             xs_d_in[0][m * 128:(m + 1) * 128,
                                        6144 + th * 1024 + tt * 512:
                                        6144 + th * 1024 + tt * 512 + 512],
                             sem="ld_x2",
                             wait=[("st_y", lb.at("st_y"))])
                v3 = lb.raw("vector",
                            (lambda E, m=m, tt=tt: E.tensor_tensor(
                                YT[:, :],
                                XN[:, m * 1024 + tt * 512:
                                   m * 1024 + tt * 512 + 512],
                                XT[:, :], ALU.add)),
                            wait=[("ve_lnh", ve_lnh), ("ld_x2", ldx)],
                            sem="ve_c", inc=1)
                lb.dma("sync",
                       y_out[m * 128:(m + 1) * 128,
                             th * 1024 + tt * 512:th * 1024 + tt * 512 + 512],
                       YT[:, :], sem="st_y", wait=[("ve_c", v3)])
    if MAMBA_PHASE < 11:
        for k in range(KCH):
            lb.dma("sync", y_out[k * 128:(k + 1) * 128, :],
                   A0[:, 0:T] if False else XN[:, 0:1024].bitcast(F32)[:, 0:512].to_broadcast([128, T]) if False else A0[:, 0:T],
                   sem="st_y",
                   wait=[(nm, lb.at(nm)) for nm in
                         ("ve_c", "ac_c", "mm_c", "mm_p", "ve_lnm", "ld_b",
                          "ld_x2", "mm_z", "mm_o", "ve_lnf", "ve_lnh",
                          "mm_f")] if k == 0 else None)
    lb.raw("sync", lambda E: E.nop(), wait=[("st_y", lb.at("st_y"))])
    return lb.finalize()


def mamba_host_inputs(inputs):
    out = {}
    for d, pfx in enumerate(("f", "b")):
        ip = inputs[f"L1_attn_a_{pfx}_in_proj_w"]
        ipt = np.zeros((D, 2432), np.float32)
        ipt[:, 0:2048] = ip[2048:4096].T
        ipt[:, 2048:2112] = ip[4096:4160].T
        ipt[:, 2176:2240] = ip[4160:4224].T
        ipt[:, 2304:2336] = ip[4224:4256].T
        out[f"ipT{d}"] = ipt
        out[f"wzT{d}"] = np.ascontiguousarray(ip[0:2048].T)
        cwo = inputs[f"L1_attn_a_{pfx}_conv_w"][:, 0, :]
        cbo = inputs[f"L1_attn_a_{pfx}_conv_b"]
        cw = np.zeros((18 * 128, 4), np.float32)
        cb = np.zeros((18 * 128,), np.float32)
        cw[0:2048] = cwo[:, 0:2048].T
        cb[0:2048] = cbo[0:2048]
        cw[16 * 128:16 * 128 + 64] = cwo[:, 2048:2112].T
        cb[16 * 128:16 * 128 + 64] = cbo[2048:2112]
        cw[17 * 128:17 * 128 + 64] = cwo[:, 2112:2176].T
        cb[17 * 128:17 * 128 + 64] = cbo[2112:2176]
        cwp = np.zeros((128, 18 * 4), np.float32)
        for c in range(18):
            for j in range(4):
                cwp[:, c * 4 + j] = cw[c * 128:(c + 1) * 128, j]
        out[f"cw{d}"] = cwp
        out[f"cb{d}"] = pack_pf(cb, 128)
        out[f"dtb{d}"] = np.ascontiguousarray(
            inputs[f"L1_attn_a_{pfx}_dt_bias"].reshape(32, 1))
        out[f"alog{d}"] = np.ascontiguousarray(
            inputs[f"L1_attn_a_{pfx}_A_log"].reshape(32, 1))
        out[f"drow{d}"] = np.ascontiguousarray(
            inputs[f"L1_attn_a_{pfx}_D"].reshape(1, 32))
        out[f"rmsw{d}"] = pack_pf(inputs[f"L1_attn_a_{pfx}_rms_w"])
        out[f"opT{d}"] = np.ascontiguousarray(
            inputs[f"L1_attn_a_{pfx}_out_proj_w"].T)
    out["pwT"] = np.ascontiguousarray(inputs["L1_attn_a_proj_w"].T)
    out["pbias"] = pack_pf(inputs["L1_attn_a_proj_b"])
    out["lnw"] = pack_pf(inputs["L1_attn_a_ln_w"])
    out["lnb"] = pack_pf(inputs["L1_attn_a_ln_b"])
    return out


def run_mamba(x_slices, inputs):
    nc = get_nc("mamba", build_mamba)
    const = mamba_host_inputs(inputs)
    in_maps = []
    for c in range(NCORES):
        b, s = core_bs(c)
        xb_full = np.concatenate([x_slices[2 * ss + b] for ss in range(4)],
                                 axis=1)
        lr = 2048 * (s + 1)
        xf = np.zeros((D, 8192), np.float32)
        xf[:, 8192 - lr:] = xb_full[:, 0:lr]
        mf = np.zeros((1, 8192), np.float32)
        mf[:, 8192 - lr:] = 1.0
        lbk = 8192 - 2048 * s
        xb_ = np.zeros((D, 8192), np.float32)
        xb_[:, 8192 - lbk:] = xb_full[:, 2048 * s:][:, ::-1]
        mb = np.zeros((1, 8192), np.float32)
        mb[:, 8192 - lbk:] = 1.0
        in_maps.append(dict(const,
                            xs_f=np.ascontiguousarray(xf),
                            xs_b=np.ascontiguousarray(xb_),
                            mask_f=mf, mask_b=mb))
    res = run_bass_kernel_spmd(nc, in_maps, core_ids=list(range(NCORES)))
    return [res.results[c]["y_out"] for c in range(NCORES)]




TIT = 2560
SH_FFN = [0, 0, 0, 0, -1, -1, -1, -1]


def build_ffn0(with_resid=False):
    lb = LB("ffn0r" if with_resid else "ffn0")
    x_it = lb.din("x_it", [D, 2564])
    wqkv = lb.din("wqkv", [D, 3 * D])
    wo = lb.din("wo", [D, D])
    w1p = lb.din("w1p", [D, 5632])
    w2p = lb.din("w2p", [2816, D])
    b1p = lb.din("b1p", [128, 44])
    b2p = lb.din("b2p", [128, KCH])
    lng = lb.din("lng", [128, KCH])
    lnb = lb.din("lnb", [128, KCH])
    qs2 = lb.din("qs2", [128, 1])
    ks2 = lb.din("ks2", [128, 1])
    csq = lb.din("csq", [128, 2564])
    ssq = lb.din("ssq", [128, 2564])
    csk = lb.din("csk", [128, 2564])
    ssk = lb.din("ssk", [128, 2564])
    am = lb.din("am", [128, 4])
    if with_resid:
        resid = lb.din("resid", [D, T])
    y_out = lb.dout("y_out", [D, T])
    h3_d = lb.dtmp("h3_d", [22, 128, T], BF16)
    x1_d = lb.dtmp("x1_d", [KCH, 128, TIT], F32)

    ARN = lb.sb("ARN", [128, 40992], BF16)
    XB = ARN[:, 0:KCH * 2564]
    CTX0 = 20512
    X1SB = ARN[:, :].bitcast(F32)[:, 0:KCH * TIT]
    XN1 = lb.sb("XN1", [128, KCH * T], BF16)
    QKVS = XN1
    TAB = lb.sb("TAB", [128, 4 * 2564], BF16)
    AW = lb.sb("AW", [128, 11264], BF16)
    SCR = lb.sb("SCR", [128, 2 * KCH * 512], BF16)
    SCRF = SCR[:, :].bitcast(F32)
    STATS = lb.sb("STATS", [1, 5 * 512], F32)
    ONES = lb.sb("ONES", [128, 1], BF16)
    ONESR = lb.sb("ONESR", [1, 128], BF16)
    BCO = lb.sb("BCO", [1, 128], F32)
    EPS = lb.sb("EPS", [1, 1], F32)
    ZCOL = lb.sb("ZCOL", [128, 1], F32)
    ID128 = lb.sb("ID128", [128, 128], BF16)
    ID64D = lb.sb("ID64D", [128, 64], BF16)
    HSELK = lb.sb("HSELK", [128, 2], BF16)
    HSELB2 = lb.sb("HSELB2", [2, 128], BF16)
    P2 = lb.sb("P2", [128, 128], BF16)
    P2Q = lb.sb("P2Q", [128, 128], BF16)
    P2K = lb.sb("P2K", [128, 128], BF16)
    M0T = SCR[:, 4096:5120]
    M0J = SCR[:, 5120:6144]
    AMS = lb.sb("AMS", [128, 4], F32)
    QSC = lb.sb("QSC", [128, 1], F32)
    KSC = lb.sb("KSC", [128, 1], F32)
    RW = lb.sb("RW", [2, 512], F32)
    RB = lb.sb("RB", [2, 512], BF16)
    MCOL = lb.sb("MCOL", [128, 2], BF16)
    NEGM = lb.sb("NEGM", [1, 256], BF16)
    PT = lb.sb("PT", [128, 4 * 256], BF16)
    VT = lb.sb("VT", [128, 4 * 65], BF16)
    RL = lb.sb("RL", [1, 256], F32)
    OSB = lb.sb("OSB", [64, 256], F32)
    QN = lb.sb("QN", [128, 512], BF16)
    T1 = lb.sb("T1", [128, 2048], BF16)
    T2 = lb.sb("T2", [128, 2 * 2048], BF16)
    XT = SCRF[:, 1024:1536]
    YT = SCRF[:, 1536:2048]
    LNG = lb.sb("LNG", [128, KCH], F32)
    LNB = lb.sb("LNB", [128, KCH], F32)
    B1P = lb.sb("B1P", [128, 44], F32)
    B2P = lb.sb("B2P", [128, KCH], F32)
    pb = [lb.psb(f"pb{i}") for i in range(8)]
    TRD = pb[1][:, :].bitcast(BF16)

    for s0 in ("ve_c", "ac_c", "mm_c", "mm_qv", "mm_wo", "mm_w1", "mm_w2",
               "st_y", "st_h", "st_x1", "ld_x2", "ld_h"):
        lb.cnt.setdefault(s0, 0)

    lb.raw("vector", lambda E: E.memset(ONES[:, :], 1.0), sem="ve_i", inc=1)
    lb.raw("vector", lambda E: E.memset(ONESR[:, :], 1.0), sem="ve_i", inc=1)
    lb.raw("vector", lambda E: E.memset(BCO[:, :], 1.0), sem="ve_i", inc=1)
    lb.raw("vector", lambda E: E.memset(EPS[:, :], 1e-5), sem="ve_i", inc=1)
    lb.raw("vector", lambda E: E.memset(ZCOL[:, :], 0.0), sem="ve_i", inc=1)
    for g in range(4):
        lb.raw("vector", (lambda E, g=g: E.memset(
            VT[:, g * 65 + 64:g * 65 + 65], 1.0)), sem="ve_i", inc=1)
    for c in range(KCH):
        lb.raw("vector", (lambda E, c=c: E.memset(
            ARN[:, CTX0 + c * TIT + 2304:CTX0 + (c + 1) * TIT], 0.0)),
            sem="ve_i", inc=1)
    lb.dma("sync", ID128[:, :], lb.nc.inline_tensor(
        np.eye(128, dtype=BFD), "f_id128").ap(), sem="ld")
    lb.dma("sync", ID64D[:, :], lb.nc.inline_tensor(
        np.concatenate([np.eye(64, dtype=BFD)] * 2, 0), "f_id64d").ap(),
        sem="ld")
    hsk = np.zeros((128, 2), dtype=BFD)
    hsk[0:64, 0] = 1
    hsk[64:128, 1] = 1
    lb.dma("sync", HSELK[:, :], lb.nc.inline_tensor(hsk, "f_hselk").ap(),
           sem="ld")
    hsb2 = np.zeros((2, 128), dtype=BFD)
    hsb2[0, 0:64] = 1
    hsb2[1, 64:128] = 1
    lb.dma("sync", HSELB2[:, :], lb.nc.inline_tensor(hsb2, "f_hselb").ap(),
           sem="ld")
    p2 = np.zeros((128, 128), dtype=BFD)
    for hh in range(2):
        o = hh * 64
        for dd in range(32):
            p2[o + dd + 32, o + dd] = -1
            p2[o + dd, o + dd + 32] = 1
    lb.dma("sync", P2[:, :], lb.nc.inline_tensor(p2, "f_p2").ap(), sem="ld")
    jj = np.arange(512)[:, None]
    ii = np.arange(256)[None, :]
    band = np.where((jj >= ii) & (jj <= ii + 256), 0.0, -1e30).astype(BFD)
    m0j = np.ascontiguousarray(band.reshape(4, 128, 256).transpose(1, 0, 2)
                               .reshape(128, 1024))
    lb.dma("sync", M0J[:, :], lb.nc.inline_tensor(m0j, "f_m0j").ap(),
           sem="ld")
    m0t = np.ascontiguousarray(band.T.reshape(2, 128, 512).transpose(1, 0, 2)
                               .reshape(128, 1024))
    lb.dma("sync", M0T[:, :], lb.nc.inline_tensor(m0t, "f_m0t").ap(),
           sem="ld")
    lb.dma("sync", AMS[:, :], am[:, :], sem="ld")
    lb.dma("sync", QSC[:, :], qs2[:, :], sem="ld")
    lb.dma("sync", KSC[:, :], ks2[:, :], sem="ld")
    lb.dma("sync", LNG[:, :], lng[:, :], sem="ld")
    lb.dma("sync", LNB[:, :], lnb[:, :], sem="ld")
    lb.dma("sync", B1P[:, :], b1p[:, :], sem="ld")
    lb.dma("sync", B2P[:, :], b2p[:, :], sem="ld")
    lb.dma("gpsimd", TAB[:, 0:2564], csq[:, :], sem="ldg")
    lb.dma("gpsimd", TAB[:, 2564:2 * 2564], ssq[:, :], sem="ldg")
    lb.dma("gpsimd", TAB[:, 2 * 2564:3 * 2564], csk[:, :], sem="ldg")
    lb.dma("gpsimd", TAB[:, 3 * 2564:4 * 2564], ssk[:, :], sem="ldg")
    ld_all = lb.at("ld")
    ldg_all = lb.at("ldg")
    lb.raw("vector", lambda E: E.tensor_scalar_mul(
        TAB[:, 0:2564], TAB[:, 0:2564], QSC[:, 0:1]),
        wait=[("ld", ld_all), ("ldg", ldg_all)], sem="ve_i", inc=1)
    lb.raw("vector", lambda E: E.tensor_scalar_mul(
        TAB[:, 2 * 2564:3 * 2564], TAB[:, 2 * 2564:3 * 2564], KSC[:, 0:1]),
        sem="ve_i", inc=1)
    lb.raw("vector", lambda E: E.tensor_scalar_mul(
        P2Q[:, :], P2[:, :], QSC[:, 0:1]), sem="ve_i", inc=1)
    lb.raw("vector", lambda E: E.tensor_scalar_mul(
        P2K[:, :], P2[:, :], KSC[:, 0:1]), sem="ve_i", inc=1)
    v_init = lb.at("ve_i")
    for k in range(KCH):
        lb.dma("gpsimd", XB[:, k * 2564:(k + 1) * 2564],
               x_it[k * 128:(k + 1) * 128, :], sem="ldx")
    ld_x = lb.at("ldx")

    q_sl = QKVS[:, 0:TIT]
    k_sl = QKVS[:, TIT:2 * TIT]
    v_sl = QKVS[:, 2 * TIT:3 * TIT]

    for c in range(KCH):
        p0c = 0

        def evac_qkv(m, tt, bank, mmval,
                     war=[("mm_c", lb.at("mm_c")), ("ac_c", lb.at("ac_c")),
                          ("ve_c", lb.at("ve_c"))]):
            a = lb.raw("vector",
                       (lambda E, m=m, tt=tt, bank=bank: E.tensor_copy(
                           QKVS[:, m * TIT + tt * 512:m * TIT + tt * 512 + 512],
                           bank[:, :])),
                       wait=[("mm_qv", mmval)] + war, sem="ve_c", inc=1)
            return ("ve_c", a)

        wr = [("ldx", ld_x), ("ve_i", v_init)] if c == 0 else []
        emit_proj(lb, "qv", wqkv, AW, XB, D, 384, TIT, evac_qkv, pb[0:5],
                  m_panel=1, rhs_col0=4, rhs_stride=2564, rhs_shift=SH_FFN,
                  wait_rhs=wr, w_m0=c * 384)

        for sl, tb0, pm in ((q_sl, 0, P2Q), (k_sl, 2 * 2564, P2K)):
            a_sq = lb.raw("scalar",
                          (lambda E, sl=sl: E.activation(
                              SCR[:, 0:TIT], sl, AF.Square)),
                          wait=[("ve_c", lb.at("ve_c"))], sem="ac_c", inc=1)
            for tt in range(5):
                mm_s = lb.raw("tensor",
                              (lambda E, tt=tt: E.matmul(
                                  pb[5][0:2, :], HSELK[:, :],
                                  SCR[:, tt * 512:(tt + 1) * 512],
                                  start=True, stop=True,
                                  skip_group_check=True)),
                              wait=[("ac_c", a_sq),
                                    ("ve_c", lb.at("ve_c"))],
                              sem="mm_c", inc=1)
                v_r = lb.raw("vector",
                             (lambda E: E.tensor_copy(RW[:, :],
                                                      pb[5][0:2, :])),
                             wait=[("mm_c", mm_s)], sem="ve_c", inc=1)
                a_s = lb.raw("scalar",
                             (lambda E: E.activation(RW[:, :], RW[:, :],
                                                     AF.Sqrt)),
                             wait=[("ve_c", v_r)], sem="ac_c", inc=1)
                lb.raw("vector",
                       (lambda E: E.tensor_scalar_max(RW[:, :], RW[:, :],
                                                      1e-12)),
                       wait=[("ac_c", a_s)], sem="ve_c", inc=1)
                lb.raw("vector",
                       (lambda E: E.reciprocal(RW[:, :], RW[:, :])),
                       sem="ve_c", inc=1)
                v_rb = lb.raw("vector",
                              (lambda E: E.tensor_copy(RB[:, :], RW[:, :])),
                              sem="ve_c", inc=1)
                mm_b = lb.raw("tensor",
                              (lambda E: E.matmul(pb[6][:, :], HSELB2[:, :],
                                                  RB[:, :], start=True,
                                                  stop=True,
                                                  skip_group_check=True)),
                              wait=[("ve_c", v_rb)], sem="mm_c", inc=1)
                v_qn = lb.raw("vector",
                              (lambda E, sl=sl, tt=tt: E.tensor_tensor(
                                  QN[:, :], sl[:, tt * 512:(tt + 1) * 512],
                                  pb[6][:, :], ALU.mult)),
                              wait=[("mm_c", mm_b)], sem="ve_c", inc=1)
                mm_r = lb.raw("tensor",
                              (lambda E, pm=pm: E.matmul(
                                  pb[7][:, :], pm[:, :], QN[:, :],
                                  start=True, stop=True,
                                  skip_group_check=True)),
                              wait=[("ve_c", v_qn)], sem="mm_c", inc=1)
                v_o1 = lb.raw("vector",
                              (lambda E, tt=tt, tb0=tb0: E.tensor_tensor(
                                  SCR[:, tt * 512:(tt + 1) * 512],
                                  TAB[:, tb0 + 2564 + 4 + tt * 512:
                                      tb0 + 2564 + 4 + tt * 512 + 512],
                                  pb[7][:, :], ALU.mult)),
                              wait=[("mm_c", mm_r)], sem="ve_c", inc=1)
                lb.raw("vector",
                       (lambda E, sl=sl, tt=tt, tb0=tb0: E.tensor_tensor(
                           sl[:, tt * 512:(tt + 1) * 512],
                           QN[:, :],
                           TAB[:, tb0 + 4 + tt * 512:tb0 + 4 + tt * 512 + 512],
                           ALU.mult)),
                       sem="ve_c", inc=1)
                lb.raw("vector",
                       (lambda E, sl=sl, tt=tt: E.tensor_add(
                           sl[:, tt * 512:(tt + 1) * 512],
                           sl[:, tt * 512:(tt + 1) * 512],
                           SCR[:, tt * 512:(tt + 1) * 512])),
                       sem="ve_c", inc=1)
        ve_rope = lb.at("ve_c")

        for hh in range(2):
            p0 = hh * 64
            for w in range(1, 10):
                q0 = 256 * w
                k0 = q0 - 256
                for it in range(2):
                    wv = [("ve_c", lb.at("ve_c"))]
                    mm1 = lb.raw("tensor",
                                 (lambda E, it=it, q0=q0, k0=k0, p0=p0:
                                  E.matmul(
                                      pb[0][:, :],
                                      q_sl[p0:p0 + 64, q0 + it * 128:
                                           q0 + it * 128 + 128],
                                      k_sl[p0:p0 + 64, k0:k0 + 512],
                                      start=True, stop=True,
                                      skip_group_check=True)),
                                 wait=wv, sem="mm_c", inc=1)
                    v_b1 = lb.raw("vector",
                                  (lambda E, it=it: E.tensor_add(
                                      SCRF[:, 0:512], pb[0][:, :],
                                      M0T[:, it * 512:(it + 1) * 512])),
                                  wait=[("mm_c", mm1)], sem="ve_c", inc=1)
                    v_m = lb.raw("vector",
                                 (lambda E, it=it: E.reduce_max(
                                     MCOL[:, it:it + 1], SCRF[:, 0:512],
                                     axis=AX.X, negate=True)),
                                 sem="ve_c", inc=1)
                    lb.raw("tensor",
                           (lambda E, it=it: E.matmul(
                               pb[1][0:1, it * 128:(it + 1) * 128],
                               MCOL[:, it:it + 1], ID128[:, :],
                               start=True, stop=True,
                               skip_group_check=True)),
                           wait=[("ve_c", v_m)], sem="mm_c", inc=1)
                m_rows = lb.at("mm_c")
                v_ng = lb.raw("vector",
                              (lambda E: E.tensor_copy(NEGM[0:1, :],
                                                       pb[1][0:1, 0:256])),
                              wait=[("mm_c", m_rows)], sem="ve_c", inc=1)
                for jt in range(4):
                    lb.raw("tensor",
                           (lambda E, jt=jt, q0=q0, k0=k0, p0=p0: E.matmul(
                               pb[2 + jt][:, 0:256],
                               k_sl[p0:p0 + 64,
                                    k0 + jt * 128:k0 + jt * 128 + 128],
                               q_sl[p0:p0 + 64, q0:q0 + 256],
                               start=True, stop=False,
                               skip_group_check=True)),
                           wait=([("ac_c", lb.at("ac_c"))]
                                 if jt == 0 else None))
                    mm2 = lb.raw("tensor",
                                 (lambda E, jt=jt: E.matmul(
                                     pb[2 + jt][:, 0:256], ONESR[0:1, :],
                                     NEGM[0:1, :], start=False, stop=True,
                                     skip_group_check=True)),
                                 wait=([("ve_c", v_ng)] if jt == 0 else None),
                                 sem="mm_c", inc=1)
                    v_bm = lb.raw("vector",
                                  (lambda E, jt=jt: E.tensor_add(
                                      SCRF[:, 512:768], pb[2 + jt][:, 0:256],
                                      M0J[:, jt * 256:(jt + 1) * 256])),
                                  wait=[("mm_c", mm2)], sem="ve_c", inc=1)
                    lb.raw("scalar",
                           (lambda E, jt=jt, w=w: E.activation(
                               PT[:, jt * 256:(jt + 1) * 256],
                               SCRF[:, 512:768], AF.Exp,
                               bias=(AMS[:, jt:jt + 1] if w == 2
                                     else ZCOL[:, 0:1]))),
                           wait=[("ve_c", v_bm)], sem="ac_c", inc=1)
                ac_pt = lb.at("ac_c")
                for jt in range(4):
                    wv = [("ve_c", lb.at("ve_c"))] if jt == 0 else None
                    mm3 = lb.raw("tensor",
                                 (lambda E, jt=jt, k0=k0, p0=p0: E.transpose(
                                     TRD[:, jt * 64:(jt + 1) * 64],
                                     v_sl[p0:p0 + 64,
                                          k0 + jt * 128:k0 + jt * 128 + 128],
                                     ID64D[p0:p0 + 64, :])),
                                 wait=wv, sem="mm_c", inc=1)
                    lb.raw("vector",
                           (lambda E, jt=jt: E.tensor_copy(
                               VT[:, jt * 65:jt * 65 + 64],
                               TRD[:, jt * 64:(jt + 1) * 64])),
                           wait=[("mm_c", mm3)], sem="ve_c", inc=1)
                v_vt = lb.at("ve_c")
                for jt in range(4):
                    wv = ([("ve_c", v_vt), ("ac_c", ac_pt)]
                          if jt == 0 else None)
                    lb.raw("tensor",
                           (lambda E, jt=jt: E.matmul(
                               pb[6][0:65, 0:256],
                               VT[:, jt * 65:jt * 65 + 65],
                               PT[:, jt * 256:(jt + 1) * 256],
                               start=(jt == 0), stop=(jt == 3),
                               skip_group_check=True)),
                           wait=wv, sem="mm_c" if jt == 3 else None,
                           inc=1 if jt == 3 else None)
                m_out = lb.at("mm_c")
                v_rl = lb.raw("vector",
                              (lambda E: E.reciprocal(RL[0:1, :],
                                                      pb[6][64:65, 0:256])),
                              wait=[("mm_c", m_out)], sem="ve_c", inc=1)
                a_o = lb.raw("scalar",
                             (lambda E: E.activation(OSB[:, :],
                                                     pb[6][0:64, 0:256],
                                                     AF.Copy)),
                             wait=[("mm_c", m_out)], sem="ac_c", inc=1)
                mm_bc = lb.raw("tensor",
                               (lambda E: E.matmul(pb[7][0:64, 0:256],
                                                   BCO[0:1, 0:64],
                                                   RL[0:1, :], start=True,
                                                   stop=True,
                                                   skip_group_check=True)),
                               wait=[("ve_c", v_rl), ("ac_c", a_o)],
                               sem="mm_c", inc=1)
                lb.raw("vector",
                       (lambda E, c=c, w=w, p0=p0: E.tensor_tensor(
                           ARN[p0:p0 + 64,
                               CTX0 + c * TIT + (w - 1) * 256:
                               CTX0 + c * TIT + (w - 1) * 256 + 256],
                           OSB[:, :], pb[7][0:64, 0:256], ALU.mult)),
                       wait=[("mm_c", mm_bc)], sem="ve_c", inc=1)

    ve_ctx = lb.at("ve_c")
    lb.wbuf_free.clear()

    def evac_x1(m, tt, bank, mmval):
        if 260 + tt * 512 + 512 <= 2564:
            src = x_it[m * 128:(m + 1) * 128,
                       260 + tt * 512:260 + tt * 512 + 512]
            dst = XT[:, :]
        else:
            src = x_it[m * 128:(m + 1) * 128, 2308:2564]
            dst = XT[:, 0:256]
        ldx = lb.dma("sync", dst, src, sem="ld_x2",
                     wait=[("ve_c", lb.at("ve_c"))])
        v1 = lb.raw("vector",
                    (lambda E, bank=bank: E.tensor_add(YT[:, :], bank[:, :],
                                                       XT[:, :])),
                    wait=[("mm_wo", mmval), ("ld_x2", ldx),
                          ("st_x1", lb.at("st_x1"))],
                    sem="ve_c", inc=1)
        lb.dma("sync", x1_d[m, :, tt * 512:(tt + 1) * 512], YT[:, :],
               sem="st_x1", wait=[("ve_c", v1)])
        return ("ve_c", v1)

    emit_proj(lb, "wo", wo, AW, ARN, D, D, TIT, evac_x1, pb[0:5],
              rhs_col0=CTX0, rhs_stride=TIT, m_panel=4,
              wait_rhs=[("ve_c", ve_ctx)],
              wait_w=[("mm_qv", lb.at("mm_qv"))])
    st_x1 = lb.at("st_x1")
    mm_wo_all = lb.at("mm_wo")

    for m in range(KCH):
        lb.dma("sync", X1SB[:, m * TIT:(m + 1) * TIT], x1_d[m],
               sem="ld_x2",
               wait=([("st_x1", st_x1), ("mm_wo", mm_wo_all)]
                     if m == 0 else None))
    ld_x1 = lb.at("ld_x2")

    ve_ln, ac_ln = emit_ln3(lb, "lnf", X1SB, XN1, SCR, STATS, ONES, BCO,
                            pb[0:4],
                            wait_x=[("ld_x2", ld_x1), ("ld", ld_all)],
                            Tn=T, eps_ap=EPS[0:1, 0:1], x_col0=256,
                            xn_col0=0, stride=TIT, xn_stride=T,
                            gbias=(LNG, LNB), stats_bufs=1, x_shift=SH_FFN)

    st_h_hist = {0: 0, 1: 0}

    def evac_w1(m, tt, bank, mmval):
        j = m // 2
        if m % 2 == 0:
            v = lb.raw("vector",
                       (lambda E, tt=tt, bank=bank, m=m: E.tensor_scalar(
                           T1[:, tt * 512:(tt + 1) * 512], bank[:, :],
                           1.0, B1P[:, m:m + 1], ALU.mult, ALU.add)),
                       wait=[("mm_w1", mmval), ("ld", ld_all)],
                       sem="ve_c", inc=1)
            return ("ve_c", v)
        bj = j % 2
        t2 = T2[:, bj * 2048 + tt * 512:bj * 2048 + tt * 512 + 512]
        a = lb.raw("scalar",
                   (lambda E, bank=bank, m=m, t2=t2: E.activation(
                       t2, bank[:, :], AF.Gelu, bias=B1P[:, m:m + 1])),
                   wait=[("mm_w1", mmval), ("st_h", st_h_hist[bj]),
                         ("ve_c", lb.at("ve_c"))],
                   sem="ac_c", inc=1)
        v = lb.raw("vector",
                   (lambda E, tt=tt, t2=t2: E.tensor_tensor(
                       t2, T1[:, tt * 512:(tt + 1) * 512], t2, ALU.mult)),
                   wait=[("ac_c", a)], sem="ve_c", inc=1)
        lb.dma("sync", h3_d[j, :, tt * 512:(tt + 1) * 512], t2,
               sem="st_h", wait=[("ve_c", v)])
        if tt == 3:
            st_h_hist[bj] = lb.at("st_h")
        return ("ac_c", a)

    emit_proj(lb, "w1", w1p, AW, XN1, D, 5632, T, evac_w1, pb,
              rhs_col0=0, rhs_stride=T, m_panel=4,
              wait_rhs=[("ve_lnf", ve_ln)])
    st_h_all = lb.at("st_h")
    mm_w1_all = lb.at("mm_w1")
    lb.wbuf_free.clear()

    for qt in range(4):
        for j in range(22):
            w0 = None
            if j == 0:
                w0 = ([("st_h", st_h_all), ("mm_w1", mm_w1_all)] if qt == 0
                      else [("mm_w2", lb.at("mm_w2"))])
            lb.dma("sync", XN1[:, j * 512:(j + 1) * 512],
                   h3_d[j, :, qt * 512:(qt + 1) * 512], sem="ld_h", wait=w0)
        ld_q = lb.at("ld_h")

        def evac_w2(m, tt, bank, mmval, qt=qt):
            v1 = lb.raw("vector",
                        (lambda E, bank=bank, m=m: E.tensor_scalar(
                            YT[:, :], bank[:, :], 1.0, B2P[:, m:m + 1],
                            ALU.mult, ALU.add)),
                        wait=[("mm_w2", mmval), ("st_y", lb.at("st_y")),
                              ("ld", ld_all)],
                        sem="ve_c", inc=1)
            vlast = lb.raw("vector",
                           (lambda E, m=m, qt=qt: E.tensor_tensor(
                               YT[:, :], YT[:, :],
                               X1SB[:, m * TIT + 256 + qt * 512:
                                    m * TIT + 256 + qt * 512 + 512],
                               ALU.add)),
                           sem="ve_c", inc=1)
            if with_resid:
                ldr = lb.dma("sync", XT[:, :],
                             resid[m * 128:(m + 1) * 128,
                                   qt * 512:(qt + 1) * 512],
                             sem="ld_x2", wait=[("ve_c", lb.at("ve_c"))])
                vlast = lb.raw("vector",
                               (lambda E: E.tensor_add(YT[:, :], YT[:, :],
                                                       XT[:, :])),
                               wait=[("ld_x2", ldr)], sem="ve_c", inc=1)
            lb.dma("sync", y_out[m * 128:(m + 1) * 128,
                                 qt * 512:(qt + 1) * 512], YT[:, :],
                   sem="st_y", wait=[("ve_c", vlast)])
            return ("ve_c", v1)

        emit_proj(lb, "w2", w2p, AW, XN1, 2816, D, 512, evac_w2, pb[0:4],
                  rhs_col0=0, rhs_stride=512, m_panel=2,
                  wait_rhs=[("ld_h", ld_q)],
                  wait_w=([("mm_w1", mm_w1_all)] if qt == 0 else []))

    lb.raw("sync", lambda E: E.nop(), wait=[("st_y", lb.at("st_y"))])
    return lb.finalize()


def build_ffn0_resid():
    return build_ffn0(with_resid=True)


def build_lnx():
    lb = LB("lnx")
    x_in = lb.din("x_in", [D, T])
    y_out = lb.dout("y_out", [D, T])
    A0 = lb.sb("A0", [128, KCH * T], F32)
    SCR = lb.sb("SCR", [128, 2 * KCH * 512], BF16)
    STATS = lb.sb("STATS", [1, 2 * 5 * 512], F32)
    ONES = lb.sb("ONES", [128, 1], BF16)
    BCO = lb.sb("BCO", [1, 128], F32)
    EPS = lb.sb("EPS", [1, 1], F32)
    pb = [lb.psb(f"pb{i}") for i in range(4)]
    lb.raw("vector", lambda E: E.memset(ONES[:, :], 1.0), sem="ve_i", inc=1)
    lb.raw("vector", lambda E: E.memset(BCO[:, :], 1.0), sem="ve_i", inc=1)
    lb.raw("vector", lambda E: E.memset(EPS[:, :], 1e-5), sem="ve_i", inc=1)
    v_init = lb.at("ve_i")
    ld_x = emit_load_fm(lb, A0, x_in, T, sem="ld")
    ve_ln, _ = emit_ln3(lb, "lnx", A0, A0, SCR, STATS, ONES, BCO, pb,
                        wait_x=[("ld", ld_x), ("ve_i", v_init)],
                        Tn=T, eps_ap=EPS[0:1, 0:1])
    for k in range(KCH):
        lb.dma("sync", y_out[k * 128:(k + 1) * 128, :],
               A0[:, k * T:(k + 1) * T], sem="st_y",
               wait=[("ve_lnx", ve_ln)] if k == 0 else None)
    lb.raw("sync", lambda E: E.nop(), wait=[("st_y", lb.at("st_y"))])
    return lb.finalize()


def ffn0_rope_tables(start):
    j = np.arange(2564, dtype=np.float32)
    t = np.maximum(np.float32(start - 516) + j, 0.0).astype(np.float32)
    half = np.arange(0, 64, 2, dtype=np.float32)
    inv = (1.0 / 10000.0 ** (half / 64)).astype(np.float32)
    fr = t[None, :] * inv[:, None]
    freqs = np.concatenate([fr, fr], 0)
    base = ((half + 0.4 * 64) / (1.4 * 64)).astype(np.float32)
    power = ((t - S // 2) / 128.0).astype(np.float32)
    sc = (base[:, None] ** power[None, :]).astype(np.float32)
    scale = np.concatenate([sc, sc], 0)
    cos, sin = np.cos(freqs), np.sin(freqs)

    def dup(a):
        return np.ascontiguousarray(
            np.concatenate([a, a], 0).astype(np.float32))
    return (dup(8.0 * cos * scale), dup(8.0 * sin * scale),
            dup(cos / scale), dup(sin / scale))


def ffn0_host_inputs(inputs, it):
    wqkv = np.asarray(inputs["L0_ffn_Wqkv"][it], np.float32)
    wqp = np.empty((D, 3 * D), np.float32)
    for c in range(KCH):
        for i3 in range(3):
            wqp[:, c * 384 + i3 * 128:c * 384 + i3 * 128 + 128] = \
                wqkv[:, i3 * D + c * 128:i3 * D + c * 128 + 128]
    W1 = np.asarray(inputs["L0_ffn_W1"][it], np.float32)
    b1 = np.asarray(inputs["L0_ffn_b1"][it], np.float32)
    DF = 2730
    w1p = np.zeros((D, 5632), np.float32)
    b1p = np.zeros((128, 44), np.float32)
    for jc in range(22):
        lo = jc * 128
        n = min(128, DF - lo)
        w1p[:, 256 * jc:256 * jc + n] = W1[:, lo:lo + n]
        w1p[:, 256 * jc + 128:256 * jc + 128 + n] = W1[:, DF + lo:DF + lo + n]
        b1p[0:n, 2 * jc] = b1[lo:lo + n]
        b1p[0:n, 2 * jc + 1] = b1[DF + lo:DF + lo + n]
    w2p = np.zeros((2816, D), np.float32)
    w2p[0:DF, :] = np.asarray(inputs["L0_ffn_W2"][it], np.float32)
    qs2 = np.concatenate([inputs["L0_ffn_q_scale"][it]] * 2).astype(
        np.float32)[:, None]
    ks2 = np.concatenate([inputs["L0_ffn_k_scale"][it]] * 2).astype(
        np.float32)[:, None]
    return {
        "wqkv": wqp,
        "wo": np.ascontiguousarray(inputs["L0_ffn_Wo"][it], dtype=np.float32),
        "w1p": w1p, "b1p": b1p, "w2p": w2p,
        "b2p": pack_pf(inputs["L0_ffn_b2"][it]),
        "lng": pack_pf(inputs["L0_ffn_ln_g"][it]),
        "lnb": pack_pf(inputs["L0_ffn_ln_b"][it]),
        "qs2": qs2, "ks2": ks2,
    }


def ffn0_core_inputs(xfm, c):
    b, s = core_bs(c)
    start = s * SEG
    x_it = np.zeros((D, 2564), np.float32)
    lo = start - 516
    srcl = max(lo, 0)
    x_it[:, srcl - lo:2564] = xfm[b][:, srcl:start + SEG]
    csq, ssq, csk, ssk = ffn0_rope_tables(start)
    am = np.zeros((128, 4), np.float32)
    if s == 0:
        am[:, 0:2] = -1e30
    return {"x_it": x_it, "csq": csq, "ssq": ssq, "csk": csk, "ssk": ssk,
            "am": am}


def run_ffn0_module(x_slices, inputs):
    nc_ln = get_nc("lnx", build_lnx)
    res = run_bass_kernel_spmd(
        nc_ln, [{"x_in": np.ascontiguousarray(x_slices[c])}
                for c in range(NCORES)], core_ids=list(range(NCORES)))
    xfm = [np.empty((D, S), np.float32) for _ in range(B)]
    for c in range(NCORES):
        b, s = core_bs(c)
        xfm[b][:, s * SEG:(s + 1) * SEG] = res.results[c]["y_out"]
    for it in range(6):
        last = (it == 5)
        nc = get_nc("ffn0_r" if last else "ffn0",
                    build_ffn0_resid if last else build_ffn0)
        const = ffn0_host_inputs(inputs, it)
        in_maps = []
        for c in range(NCORES):
            m = dict(const)
            m.update(ffn0_core_inputs(xfm, c))
            if last:
                m["resid"] = np.ascontiguousarray(x_slices[c])
            in_maps.append(m)
        res = run_bass_kernel_spmd(nc, in_maps, core_ids=list(range(NCORES)))
        for c in range(NCORES):
            b, s = core_bs(c)
            xfm[b][:, s * SEG:(s + 1) * SEG] = res.results[c]["y_out"]
    return [np.ascontiguousarray(xfm[core_bs(c)[0]]
                                 [:, core_bs(c)[1] * SEG:
                                  (core_bs(c)[1] + 1) * SEG])
            for c in range(NCORES)]
